# version 02
# baseline (speedup 1.0000x reference)
import numpy as np
import ml_dtypes
import concourse.bass as bass
import concourse.mybir as mybir

F32 = mybir.dt.float32
BF16 = mybir.dt.bfloat16
AF = mybir.ActivationFunctionType
ALU = mybir.AluOpType
AX = mybir.AxisListType
BF = ml_dtypes.bfloat16

NT = 4096
NTILE = NT // 128
D = 1024
HEADS, DH, WIN = 16, 64, 256
NWIN = NT // WIN
QK_SCALE = 8.0
DIM_INNER = 2730
T_FULL = 8192

NEG_BIG = -3.0e38
PI = float(np.pi)
LN1E4 = float(np.log(10000.0))


def ap(t, dims, offset=0):
    return bass.AP(t, offset, [list(d) for d in dims])


class Phase:

    def __init__(self, name):
        self.nc = bass.Bass("TRN2", target_bir_lowering=False)
        self.name = name
        self.cvals = []
        self.inputs = {}
        self.outputs = {}
        self.cstt_dram = self.inp("cstt", (128, 32))
        self.cst_sb = self.nc.alloc_sbuf_tensor("cst_sb", [128, 32], F32)

    def C(self, v):
        v = float(np.float32(v))
        if v not in self.cvals:
            assert len(self.cvals) < 32
            self.cvals.append(v)
            self.nc.const_aps.register(
                F32, v, self.cst_sb[:, len(self.cvals) - 1:len(self.cvals)])
        return v

    def cst_array(self):
        a = np.zeros((128, 32), np.float32)
        for i, v in enumerate(self.cvals):
            a[:, i] = v
        return a

    def inp(self, name, shape, dtype=F32):
        self.inputs[name] = (tuple(shape), dtype)
        return self.nc.dram_tensor(name, list(shape), dtype, kind="ExternalInput")

    def out(self, name, shape, dtype=F32):
        self.outputs[name] = (tuple(shape), dtype)
        return self.nc.dram_tensor(name, list(shape), dtype, kind="ExternalOutput")

    def load_consts(self, eng, sem):
        eng.dma_start(out=self.cst_sb[:, :], in_=self.cstt_dram[:, :]).then_inc(sem, 16)




_CHAIN_EXEMPT = {
    "dma_start", "dma_start_transpose", "wait_ge", "then_inc", "sem_inc",
    "Fori", "register", "print", "engine_nop", "nop",
}


class Ch:

    def __init__(self, eng, sem, ms, tag, start=0):
        self._e = eng
        self._sem = sem
        self._ms = ms
        self._tag = tag
        self.n = start

    def wait_ge(self, sem, v):
        if v > 0:
            self._e.wait_ge(sem, v)

    def wait_ms(self, other, name):
        v = self._ms.get(other._tag + ":" + name, 0)
        if v > 0:
            self._e.wait_ge(other._sem, v)

    def ms(self, name):
        self._ms[self._tag + ":" + name] = self.n

    def dma_start(self, *a, **k):
        return self._e.dma_start(*a, **k)

    def __getattr__(self, name):
        f = getattr(self._e, name)
        if name.startswith("_") or name in _CHAIN_EXEMPT or not callable(f):
            return f

        def wrap(*a, **k):
            inst = f(*a, **k)
            if isinstance(inst, bass.BassInstruction):
                if self.n > 0:
                    inst._wait_ge(self._sem, self.n)
                inst.then_inc(self._sem, 1)
                self.n += 1
            return inst

        return wrap


def build_twice(fn):
    ms = {}
    fn(ms)
    return fn(ms)


def host_dj():
    return np.tile(np.arange(32, dtype=np.float32)[None, :], (128, 1))


def host_tpos(t0):
    i = np.arange(NTILE)[None, :]
    p = np.arange(128)[:, None]
    return (t0 + i * 128 + p).astype(np.float32)


def _build_FA(ms):
    ph = Phase("FA")
    nc = ph.nc
    C = ph.C

    xi_bf = ph.inp("xi_bf", (1 + NT, D), BF16)
    wqkv = ph.inp("Wqkv", (D, 3 * D))
    qsks = ph.inp("qsks", (4, 32))
    tpos = ph.inp("tpos", (128, NTILE))
    dj_dram = ph.inp("dj", (128, 32))

    qtm = ph.out("qtm", (NT, D), BF16)
    ktm = ph.out("ktm", (NT, D), BF16)
    vtm = ph.out("vtm", (NT, HEADS * 65), BF16)

    NB = 2
    wsb = nc.alloc_sbuf_tensor("wsb", [128, 8 * 3072], BF16)
    tabs = nc.alloc_sbuf_tensor("tabs", [128, 8 * NTILE * 32], F32)
    tpos_sb = nc.alloc_sbuf_tensor("tpos_sb", [128, NTILE], F32)
    qsks_sb = nc.alloc_sbuf_tensor("qsks_sb", [128, 128], F32)
    dj_sb = nc.alloc_sbuf_tensor("dj_sb", [128, 32], F32)
    scr = nc.alloc_sbuf_tensor("scr", [128, 8 * NTILE * 32], F32)
    xfm = nc.alloc_sbuf_tensor("xfm", [128, NB * 8 * 128], BF16)
    qf = nc.alloc_sbuf_tensor("qf", [128, NB * 1024], F32)
    kf = nc.alloc_sbuf_tensor("kf", [128, NB * 1024], F32)
    sqs = nc.alloc_sbuf_tensor("sqs", [128, NB * 2048], F32)
    ssum = nc.alloc_sbuf_tensor("ssum", [128, NB * 96], F32)
    tmpa = nc.alloc_sbuf_tensor("tmpa", [128, 512], F32)
    tmpb = nc.alloc_sbuf_tensor("tmpb", [128, 512], F32)
    qb = nc.alloc_sbuf_tensor("qb", [128, NB * 1024], BF16)
    kb = nc.alloc_sbuf_tensor("kb", [128, NB * 1024], BF16)
    vb = nc.alloc_sbuf_tensor("vb", [128, NB * HEADS * 65], BF16)

    pqkv = nc.alloc_psum_tensor("pqkv", [128, 3072], F32)

    TT8 = NTILE * 32

    def sv(j):
        return scr[:, j * TT8:(j + 1) * TT8]

    def tab(j):
        return tabs[:, j * TT8:(j + 1) * TT8]

    with (
        nc.semaphore("s_ld0") as s_ld0,
        nc.semaphore("s_lda") as s_lda,
        nc.semaphore("s_ldb") as s_ldb,
        nc.semaphore("s_w") as s_w,
        nc.semaphore("s_mm") as s_mm,
        nc.semaphore("s_sta") as s_sta,
        nc.semaphore("s_stb") as s_stb,
        nc.semaphore("chv") as chv,
        nc.semaphore("chs") as chs,
        nc.Block() as block,
    ):
        V = {}

        @block.sync
        def _(sync):
            ph.load_consts(sync, s_ld0)
            sync.dma_start(out=tpos_sb[:, :], in_=tpos[:, :]).then_inc(s_ld0, 16)
            sync.dma_start(out=qsks_sb[:, :],
                           in_=ap(qsks, [[0, 128], [1, 128]])).then_inc(s_ld0, 16)
            sync.dma_start(out=dj_sb[:, :], in_=dj_dram[:, :]).then_inc(s_ld0, 16)
            for i in range(NTILE):
                b = i % NB
                if i >= NB:
                    sync.wait_ge(s_mm, i - NB + 1)
                sld = s_lda if b == 0 else s_ldb
                for blk in range(8):
                    roff = (1 + i * 128) if blk < 4 else (i * 128)
                    sync.dma_start(
                        out=ap(xfm, [[NB * 8 * 128, 128], [1, 128]],
                               b * 8 * 128 + blk * 128),
                        in_=ap(xi_bf, [[D, 128], [1, 128]], roff * D + blk * 128),
                        transpose=True,
                    ).then_inc(sld, 16)

        @block.scalar
        def _(sc0):
            sc = Ch(sc0, chs, ms, "s")
            V["sc"] = sc
            sc.wait_ge(s_ld0, 16 * 4)
            sc.activation(scr[:, 7 * TT8:7 * TT8 + 32], dj_sb[:, :], AF.Exp,
                          scale=C(-2.0 * LN1E4 / 64.0))
            sc.activation(scr[:, 7 * TT8 + 32:7 * TT8 + 64], dj_sb[:, :], AF.Ln,
                          scale=C(2.0 / 89.6), bias=C(25.6 / 89.6))
            sc.ms("ivf")
            sc.wait_ms(V["v"], "y") if False else None
            sc.wait_ge(chv, ms.get("v:y", 0))
            sc.activation(sv(3), sv(1), AF.Sin)
            sc.activation(sv(0), sv(2), AF.Sin)
            sc.activation(sv(5), sv(4), AF.Exp)
            sc.activation(sv(6), sv(4), AF.Exp, scale=C(-1.0))
            sc.ms("sincos")
            for i in range(NTILE):
                b = i % NB
                sc.wait_ge(s_mm, i + 1)
                if i >= NB:
                    sc.wait_ge(chv, ms.get(f"v:rot_{i - NB}", 0))
                    sc.wait_ge(s_sta if i % NB == 0 else s_stb, 48 * (i // NB))
                sc.copy(qf[:, b * 1024:(b + 1) * 1024], pqkv[:, 0:1024])
                sc.copy(kf[:, b * 1024:(b + 1) * 1024], pqkv[:, 1024:2048])
                sc.activation(ap(vb, [[NB * 1040, 128], [65, 16], [1, 64]], b * 1040),
                              pqkv[:, 2048:3072], AF.Copy)
                sc.ms(f"ev_{i}")
                sc.activation(sqs[:, b * 2048:b * 2048 + 1024],
                              qf[:, b * 1024:(b + 1) * 1024], AF.Square)
                sc.activation(sqs[:, b * 2048 + 1024:b * 2048 + 2048],
                              kf[:, b * 1024:(b + 1) * 1024], AF.Square)
                sc.ms(f"sq_{i}")
                sc.wait_ge(chv, ms.get(f"v:red_{i}", 0))
                sc.activation(ssum[:, b * 96 + 32:b * 96 + 64],
                              ssum[:, b * 96:b * 96 + 32], AF.Sqrt)
                sc.ms(f"sqrt_{i}")

        @block.vector
        def _(v0):
            v = Ch(v0, chv, ms, "v")
            V["v"] = v
            for b_ in range(NB):
                v.memset(ap(vb, [[NB * 1040, 128], [65, 16], [1, 1]], b_ * 1040 + 64), 1.0)
            v.wait_ge(s_ld0, 16 * 4)
            v.wait_ge(chs, ms.get("s:ivf", 0))
            v.tensor_tensor(sv(0), ap(tpos_sb, [[NTILE, 128], [1, NTILE], [0, 32]]),
                            ap(scr, [[8 * TT8, 128], [0, NTILE], [1, 32]], 7 * TT8),
                            ALU.mult)
            MAGIC = 12582912.0
            v.tensor_scalar(sv(1), sv(0), C(1.0 / (2 * PI)), C(MAGIC), ALU.mult, ALU.add)
            v.tensor_scalar(sv(1), sv(1), C(-MAGIC), None, ALU.add)
            v.scalar_tensor_tensor(sv(1), sv(1), C(-2 * PI), sv(0), ALU.mult, ALU.add)
            v.tensor_scalar(sv(1), sv(1), C(3.14159), C(-3.14159), ALU.min, ALU.max)
            v.tensor_scalar(sv(2), sv(0), C(PI / 2), None, ALU.add)
            v.tensor_scalar(sv(3), sv(2), C(1.0 / (2 * PI)), C(MAGIC), ALU.mult, ALU.add)
            v.tensor_scalar(sv(3), sv(3), C(-MAGIC), None, ALU.add)
            v.scalar_tensor_tensor(sv(2), sv(3), C(-2 * PI), sv(2), ALU.mult, ALU.add)
            v.tensor_scalar(sv(2), sv(2), C(3.14159), C(-3.14159), ALU.min, ALU.max)
            v.tensor_scalar(scr[:, 7 * TT8 + 64:7 * TT8 + 64 + NTILE], tpos_sb[:, :],
                            C(-float(T_FULL // 2)), C(1.0 / 128.0), ALU.add, ALU.mult)
            v.tensor_tensor(sv(4),
                            ap(scr, [[8 * TT8, 128], [1, NTILE], [0, 32]], 7 * TT8 + 64),
                            ap(scr, [[8 * TT8, 128], [0, NTILE], [1, 32]], 7 * TT8 + 32),
                            ALU.mult)
            v.ms("y")
            v.wait_ge(chs, ms.get("s:sincos", 0))
            v.tensor_tensor(sv(1), sv(0), sv(5), ALU.mult)
            v.tensor_tensor(sv(2), sv(3), sv(5), ALU.mult)
            v.tensor_tensor(sv(4), sv(0), sv(6), ALU.mult)
            v.tensor_tensor(sv(5), sv(3), sv(6), ALU.mult)
            for j, (srcp, row) in enumerate([(1, 0), (1, 1), (2, 0), (2, 1),
                                             (4, 2), (4, 3), (5, 2), (5, 3)]):
                v.tensor_tensor(tab(j), sv(srcp),
                                ap(qsks_sb, [[128, 128], [0, NTILE], [1, 32]], row * 32),
                                ALU.mult)
            for i in range(NTILE):
                b = i % NB
                v.wait_ge(chs, ms.get(f"s:sq_{i}", 0))
                v.reduce_sum(ssum[:, b * 96:b * 96 + 32],
                             ap(sqs, [[NB * 2048, 128], [64, 32], [1, 64]], b * 2048),
                             axis=AX.X)
                v.ms(f"red_{i}")
                v.wait_ge(chs, ms.get(f"s:sqrt_{i}", 0))
                v.reciprocal(ssum[:, b * 96 + 64:b * 96 + 96],
                             ssum[:, b * 96 + 32:b * 96 + 64])
                v.tensor_scalar(ssum[:, b * 96 + 64:b * 96 + 96],
                                ssum[:, b * 96 + 64:b * 96 + 96],
                                C(1.0e12), None, ALU.min)
                if i >= NB:
                    v.wait_ge(s_sta if i % NB == 0 else s_stb, 48 * (i // NB))
                v.tensor_tensor(sqs[:, b * 2048:b * 2048 + 1024],
                                qf[:, b * 1024:(b + 1) * 1024],
                                ap(ssum, [[NB * 96, 128], [1, 16], [0, 64]], b * 96 + 64),
                                ALU.mult)
                v.tensor_tensor(sqs[:, b * 2048 + 1024:b * 2048 + 2048],
                                kf[:, b * 1024:(b + 1) * 1024],
                                ap(ssum, [[NB * 96, 128], [1, 16], [0, 64]], b * 96 + 80),
                                ALU.mult)
                for (src_off, dst, tA_lo, tA_hi, tB_lo, tB_hi) in (
                        (b * 2048, qb, 0, 1, 2, 3),
                        (b * 2048 + 1024, kb, 4, 5, 6, 7)):
                    nlo = [[NB * 2048, 128], [64, 16], [1, 32]]
                    tv = [[8 * TT8, 128], [0, 16], [1, 32]]
                    ov = [[NB * 1024, 128], [64, 16], [1, 32]]
                    v.tensor_tensor(tmpa[:, :], ap(sqs, nlo, src_off),
                                    ap(tabs, tv, tA_lo * TT8 + i * 32), ALU.mult)
                    v.tensor_tensor(tmpb[:, :], ap(sqs, nlo, src_off + 32),
                                    ap(tabs, tv, tB_hi * TT8 + i * 32), ALU.mult)
                    v.tensor_tensor(ap(dst, ov, b * 1024), tmpa[:, :], tmpb[:, :],
                                    ALU.subtract)
                    v.tensor_tensor(tmpa[:, :], ap(sqs, nlo, src_off + 32),
                                    ap(tabs, tv, tA_hi * TT8 + i * 32), ALU.mult)
                    v.tensor_tensor(tmpb[:, :], ap(sqs, nlo, src_off),
                                    ap(tabs, tv, tB_lo * TT8 + i * 32), ALU.mult)
                    v.tensor_tensor(ap(dst, ov, b * 1024 + 32), tmpa[:, :], tmpb[:, :],
                                    ALU.add)
                v.ms(f"rot_{i}")

        @block.tensor
        def _(t):
            t.wait_ge(s_w, 16 * 8)
            for i in range(NTILE):
                b = i % NB
                t.wait_ge(s_lda if i % NB == 0 else s_ldb, 16 * 8 * (i // NB + 1))
                if i > 0:
                    t.wait_ge(chs, ms.get(f"s:ev_{i - 1}", 0))
                mi = None
                for n in range(6):
                    for k in range(8):
                        mi = t.matmul(
                            pqkv[:, n * 512:(n + 1) * 512],
                            xfm[:, b * 1024 + k * 128:b * 1024 + (k + 1) * 128],
                            wsb[:, k * 3072 + n * 512:k * 3072 + (n + 1) * 512],
                            start=(k == 0), stop=(k == 7))
                mi.then_inc(s_mm, 1)

        @block.gpsimd
        def _(g):
            for k in range(8):
                g.dma_start(out=wsb[:, k * 3072:(k + 1) * 3072],
                            in_=ap(wqkv, [[3 * D, 128], [1, 3 * D]], k * 128 * 3 * D)
                            ).then_inc(s_w, 16)
            for i in range(NTILE):
                b = i % NB
                g.wait_ge(chv, ms.get(f"v:rot_{i}", 0))
                g.wait_ge(chs, ms.get(f"s:ev_{i}", 0))
                sst = s_sta if b == 0 else s_stb
                g.dma_start(out=ap(qtm, [[D, 128], [1, D]], i * 128 * D),
                            in_=qb[:, b * 1024:(b + 1) * 1024]).then_inc(sst, 16)
                g.dma_start(out=ap(ktm, [[D, 128], [1, D]], i * 128 * D),
                            in_=kb[:, b * 1024:(b + 1) * 1024]).then_inc(sst, 16)
                g.dma_start(out=ap(vtm, [[1040, 128], [1, 1040]], i * 128 * 1040),
                            in_=vb[:, b * 1040:(b + 1) * 1040]).then_inc(sst, 16)
            g.wait_ge(s_sta, 48 * ((NTILE + 1) // 2))
            g.wait_ge(s_stb, 48 * (NTILE // 2))

    return ph


def build_FA():
    return build_twice(_build_FA)


def _build_FB(ms):
    ph = Phase("FB")
    nc = ph.nc
    C = ph.C

    qtm = ph.inp("qtm", (NT, D), BF16)
    ktm_h = ph.inp("ktm_h", (WIN + NT, D), BF16)
    vtm_h = ph.inp("vtm_h", (WIN + NT, HEADS * 65), BF16)
    xi = ph.inp("xi", (1 + NT, D))
    wo = ph.inp("Wo", (D, D))
    masks = ph.inp("masks", (128, 2048))
    identf = ph.inp("identf", (128, 128))
    onesrow = ph.inp("onesrow", (1, 128))

    x2o = ph.out("x2", (1 + NT, D))

    NBW = 2
    wosb = nc.alloc_sbuf_tensor("wosb", [128, 8 * 1024], BF16)
    msk = nc.alloc_sbuf_tensor("msk", [128, 2 * 1024], F32)
    idf = nc.alloc_sbuf_tensor("idf", [128, 128], F32)
    ones_sb = nc.alloc_sbuf_tensor("ones_sb", [1, 128], F32)
    qfm = nc.alloc_sbuf_tensor("qfm", [128, NBW * 8 * 256], BF16)
    kfm = nc.alloc_sbuf_tensor("kfm", [128, NBW * 8 * 512], BF16)
    vsb = nc.alloc_sbuf_tensor("vsb", [128, NBW * 4 * 1040], BF16)
    xisb = nc.alloc_sbuf_tensor("xisb", [128, NBW * 2 * 1024], F32)
    stm = nc.alloc_sbuf_tensor("stm", [128, 4096], F32)
    at = nc.alloc_sbuf_tensor("at", [128, 4096], BF16)
    mxsb = nc.alloc_sbuf_tensor("mxsb", [128, 4], F32)
    mxk = nc.alloc_sbuf_tensor("mxk", [4, 1], F32)
    negrow = nc.alloc_sbuf_tensor("negrow", [1, 4], F32)
    negmb = nc.alloc_sbuf_tensor("negmb", [128, 4], F32)
    rdr = nc.alloc_sbuf_tensor("rdr", [1, 256], F32)
    of32 = nc.alloc_sbuf_tensor("of32", [64, 256], F32)
    aout = nc.alloc_sbuf_tensor("aout", [128, 8 * 256], BF16)
    x2sb = nc.alloc_sbuf_tensor("x2sb", [128, NBW * 2 * 1024], F32)

    pst = nc.alloc_psum_tensor("pst", [128, 4096], F32)

    with (
        nc.semaphore("s_ld0") as s_ld0,
        nc.semaphore("s_lda") as s_lda,
        nc.semaphore("s_ldb") as s_ldb,
        nc.semaphore("s_w") as s_w,
        nc.semaphore("s_mm") as s_mm,
        nc.semaphore("s_sta") as s_sta,
        nc.semaphore("s_stb") as s_stb,
        nc.semaphore("chv") as chv,
        nc.semaphore("chs") as chs,
        nc.Block() as block,
    ):
        pe_n = [0]

        @block.sync
        def _(sync):
            for w in range(NWIN):
                b = w % NBW
                if w >= NBW:
                    sync.wait_ge(s_mm, ms.get(f"t:windone_{w - NBW}", 0))
                    sync.wait_ge(chv, ms.get(f"v:x2_{w - NBW}", 0))
                sld = s_lda if b == 0 else s_ldb
                for blk in range(8):
                    sync.dma_start(
                        out=ap(qfm, [[NBW * 8 * 256, 128], [1, 256]],
                               b * 8 * 256 + blk * 256),
                        in_=ap(qtm, [[D, 256], [1, 128]], w * 256 * D + blk * 128),
                        transpose=True).then_inc(sld, 16)
                for blk in range(8):
                    sync.dma_start(
                        out=ap(kfm, [[NBW * 8 * 512, 128], [1, 512]],
                               b * 8 * 512 + blk * 512),
                        in_=ap(ktm_h, [[D, 512], [1, 128]], w * 256 * D + blk * 128),
                        transpose=True).then_inc(sld, 16)
                for kt in range(4):
                    sync.dma_start(
                        out=ap(vsb, [[NBW * 4 * 1040, 128], [1, 1040]],
                               b * 4 * 1040 + kt * 1040),
                        in_=ap(vtm_h, [[1040, 128], [1, 1040]],
                               (w * 256 + kt * 128) * 1040)).then_inc(sld, 16)
                for tt in range(2):
                    sync.dma_start(
                        out=xisb[:, (b * 2 + tt) * 1024:(b * 2 + tt + 1) * 1024],
                        in_=ap(xi, [[D, 128], [1, D]],
                               (1 + w * 256 + tt * 128) * D)).then_inc(sld, 16)

        @block.gpsimd
        def _(g):
            ph.load_consts(g, s_ld0)
            g.dma_start(out=msk[:, :], in_=masks[:, :]).then_inc(s_ld0, 16)
            g.dma_start(out=idf[:, :], in_=identf[:, :]).then_inc(s_ld0, 16)
            g.dma_start(out=ones_sb[:, :], in_=onesrow[:, :]).then_inc(s_ld0, 16)
            for k in range(8):
                g.dma_start(out=wosb[:, k * 1024:(k + 1) * 1024],
                            in_=ap(wo, [[D, 128], [1, D]], k * 128 * D)).then_inc(s_w, 16)
            for w in range(NWIN):
                b = w % NBW
                sst = s_sta if b == 0 else s_stb
                g.wait_ge(chv, ms.get(f"v:x2_{w}", 0))
                for tt in range(2):
                    g.dma_start(out=ap(x2o, [[D, 128], [1, D]],
                                       (1 + w * 256 + tt * 128) * D),
                                in_=x2sb[:, (b * 2 + tt) * 1024:(b * 2 + tt + 1) * 1024]
                                ).then_inc(sst, 16)
            g.wait_ge(s_sta, 32 * ((NWIN + 1) // 2))
            g.wait_ge(s_stb, 32 * (NWIN // 2))

        @block.tensor
        def _(t):
            t.wait_ge(s_w, 16 * 8)
            t.wait_ge(s_ld0, 16 * 4)
            for w in range(NWIN):
                b = w % NBW
                t.wait_ge(s_lda if b == 0 else s_ldb, 22 * 16 * (w // NBW + 1))
                if w > 0:
                    t.wait_ge(chv, ms.get(f"v:x2_{w - 1}", 0))
                CUT = 6
                for hg in range(4):
                    if CUT < 0:
                        continue
                    if w > 0 or hg > 0:
                        t.wait_ge(chv, ms.get(f"v:msk_{w}_{hg - 1}" if hg > 0
                                              else f"v:msk_{w - 1}_3", 0))
                        t.wait_ge(chv, ms.get(f"v:div_{w}_{hg - 1}_3" if hg > 0
                                              else f"v:div_{w - 1}_3_3", 0))
                    mi = None
                    for h in range(4):
                        for kt in range(4):
                            hh = hg * 4 + h
                            pb = (hh % 2) * 64
                            mi = t.matmul(
                                pst[:, h * 1024 + kt * 256:h * 1024 + (kt + 1) * 256],
                                ap(kfm, [[NBW * 8 * 512, 64], [1, 128]],
                                   pb * NBW * 8 * 512 + b * 8 * 512
                                   + (hh // 2) * 512 + kt * 128),
                                ap(qfm, [[NBW * 8 * 256, 64], [1, 256]],
                                   pb * NBW * 8 * 256 + b * 8 * 256 + (hh // 2) * 256),
                                start=True, stop=True)
                    mi.then_inc(s_mm, 1)
                    pe_n[0] += 1
                    ms[f"t:S_{w}_{hg}"] = pe_n[0]
                    if CUT < 2:
                        continue
                    t.wait_ge(chv, ms.get(f"v:mx_{w}_{hg}", 0))
                    t.transpose(ap(pst, [[4096, 4], [1, 128]], 0), mxsb[:, :],
                                idf[:, :]).then_inc(s_mm, 1)
                    pe_n[0] += 1
                    ms[f"t:tp1_{w}_{hg}"] = pe_n[0]
                    t.wait_ge(chv, ms.get(f"v:mxk_{w}_{hg}", 0))
                    t.transpose(ap(pst, [[4096, 1], [1, 4]], 512), mxk[:, :],
                                idf[0:4, 0:4]).then_inc(s_mm, 1)
                    pe_n[0] += 1
                    ms[f"t:tp2_{w}_{hg}"] = pe_n[0]
                    t.wait_ge(chs, ms.get(f"s:neg_{w}_{hg}", 0))
                    t.matmul(pst[:, 1024:1028], ones_sb[:, :], negrow[:, :],
                             start=True, stop=True).then_inc(s_mm, 1)
                    pe_n[0] += 1
                    ms[f"t:nmb_{w}_{hg}"] = pe_n[0]
                    if CUT < 4:
                        continue
                    t.wait_ge(chs, ms.get(f"s:exp_{w}_{hg}", 0))
                    for h in range(4):
                        hh = hg * 4 + h
                        for kt in range(4):
                            mi = t.matmul(
                                ap(pst, [[4096, 65], [1, 256]], h * 256),
                                ap(vsb, [[NBW * 4 * 1040, 128], [1, 65]],
                                   b * 4 * 1040 + kt * 1040 + hh * 65),
                                at[:, h * 1024 + kt * 256:h * 1024 + (kt + 1) * 256],
                                start=(kt == 0), stop=(kt == 3))
                    mi.then_inc(s_mm, 1)
                    pe_n[0] += 1
                    ms[f"t:av_{w}_{hg}"] = pe_n[0]
                    if CUT < 5:
                        continue
                    for h in range(4):
                        t.wait_ge(chv, ms.get(f"v:rd_{w}_{hg}_{h}", 0))
                        if h > 0:
                            t.wait_ge(chv, ms.get(f"v:div_{w}_{hg}_{h - 1}", 0))
                        elif hg > 0:
                            t.wait_ge(chv, ms.get(f"v:div_{w}_{hg - 1}_3", 0))
                        t.matmul(ap(pst, [[4096, 64], [1, 256]], 1024 + 256),
                                 ones_sb[:, 0:64],
                                 rdr[:, :], start=True, stop=True).then_inc(s_mm, 1)
                        pe_n[0] += 1
                        ms[f"t:bc_{w}_{hg}_{h}"] = pe_n[0]
                if CUT < 6:
                    continue
                t.wait_ge(chv, ms.get(f"v:div_{w}_3_3", 0))
                for tt in range(2):
                    mi = None
                    for n in range(2):
                        for k in range(8):
                            mi = t.matmul(
                                pst[:, 2048 + (tt * 2 + n) * 512:
                                    2048 + (tt * 2 + n + 1) * 512],
                                aout[:, k * 256 + tt * 128:k * 256 + tt * 128 + 128],
                                wosb[:, k * 1024 + n * 512:k * 1024 + (n + 1) * 512],
                                start=(k == 0), stop=(k == 7))
                    mi.then_inc(s_mm, 1)
                    pe_n[0] += 1
                    ms[f"t:wo_{w}_{tt}"] = pe_n[0]
                pe_n[0] = pe_n[0]
                ms[f"t:windone_{w}"] = pe_n[0]

        @block.vector
        def _(v0):
            v = Ch(v0, chv, ms, "v")
            CUT = 6
            for w in range(NWIN):
                b = w % NBW
                v.wait_ge(s_lda if b == 0 else s_ldb, 22 * 16 * (w // NBW + 1))
                for hg in range(4):
                    if CUT < 1:
                        continue
                    v.wait_ge(s_mm, ms.get(f"t:S_{w}_{hg}", 0))
                    mplane = 0 if w == 0 else 1
                    v.scalar_tensor_tensor(
                        stm[:, :],
                        pst[:, :],
                        C(QK_SCALE),
                        ap(msk, [[2048, 128], [0, 4], [256, 4], [1, 256]],
                           mplane * 1024),
                        ALU.mult, ALU.add)
                    v.ms(f"msk_{w}_{hg}")
                    if CUT < 2:
                        continue
                    v.reduce_max(mxsb[:, :],
                                 ap(stm, [[4096, 128], [1024, 4], [1, 1024]]),
                                 axis=AX.X)
                    v.ms(f"mx_{w}_{hg}")
                    v.wait_ge(s_mm, ms.get(f"t:tp1_{w}_{hg}", 0))
                    v.reduce_max(mxk[:, :], ap(pst, [[4096, 4], [1, 128]], 0),
                                 axis=AX.X)
                    v.ms(f"mxk_{w}_{hg}")
                    if CUT < 5:
                        continue
                    v.wait_ge(s_mm, ms.get(f"t:av_{w}_{hg}", 0))
                    for h in range(4):
                        hh = hg * 4 + h
                        v.reciprocal(rdr[:, :],
                                     ap(pst, [[4096, 1], [1, 256]],
                                        64 * 4096 + h * 256))
                        v.ms(f"rd_{w}_{hg}_{h}")
                        v.wait_ge(s_mm, ms.get(f"t:bc_{w}_{hg}_{h}", 0))
                        v.wait_ge(chs, ms.get(f"s:oc_{w}_{hg}_{h}", 0))
                        v.tensor_tensor(
                            ap(aout, [[8 * 256, 64], [1, 256]],
                               ((hh % 2) * 64) * 8 * 256 + (hh // 2) * 256),
                            of32[:, :],
                            ap(pst, [[4096, 64], [1, 256]], 1024 + 256),
                            ALU.mult)
                        v.ms(f"div_{w}_{hg}_{h}")
                for tt in range(2):
                    if w >= NBW and tt == 0:
                        v.wait_ge(s_sta if b == 0 else s_stb, 32 * (w // NBW))
                    if CUT < 6:
                        v.memset(x2sb[:, (b * 2 + tt) * 1024:(b * 2 + tt + 1) * 1024], 0.0)
                        continue
                    v.wait_ge(s_mm, ms.get(f"t:wo_{w}_{tt}", 0))
                    v.tensor_tensor(x2sb[:, (b * 2 + tt) * 1024:(b * 2 + tt + 1) * 1024],
                                    pst[:, 2048 + tt * 1024:2048 + (tt + 1) * 1024],
                                    xisb[:, (b * 2 + tt) * 1024:(b * 2 + tt + 1) * 1024],
                                    ALU.add)
                v.ms(f"x2_{w}")

        @block.scalar
        def _(sc0):
            sc = Ch(sc0, chs, ms, "s")
            CUT = 6
            for w in range(NWIN):
                for hg in range(4):
                    if CUT < 2:
                        continue
                    sc.wait_ge(s_mm, ms.get(f"t:tp2_{w}_{hg}", 0))
                    sc.copy(negrow[:, :], ap(pst, [[4096, 1], [1, 4]], 512))
                    sc.mul(negrow[:, :], negrow[:, :], C(-1.0))
                    sc.ms(f"neg_{w}_{hg}")
                    sc.wait_ge(s_mm, ms.get(f"t:nmb_{w}_{hg}", 0))
                    sc.copy(negmb[:, :], pst[:, 1024:1028])
                    if CUT < 3:
                        continue
                    for h in range(4):
                        sc.activation(at[:, h * 1024:(h + 1) * 1024],
                                      stm[:, h * 1024:(h + 1) * 1024],
                                      AF.Exp, bias=negmb[:, h:h + 1])
                    sc.ms(f"exp_{w}_{hg}")
                    if CUT < 5:
                        continue
                    sc.wait_ge(s_mm, ms.get(f"t:av_{w}_{hg}", 0))
                    for h in range(4):
                        sc.copy(of32[:, :], ap(pst, [[4096, 64], [1, 256]], h * 256))
                        sc.ms(f"oc_{w}_{hg}_{h}")
                        sc.wait_ge(chv, ms.get(f"v:div_{w}_{hg}_{h}", 0))
                    sc.ms(f"oev_{w}_{hg}")

    return ph


def build_FB():
    return build_twice(_build_FB)


def host_masks(first_is_start):
    out = np.zeros((128, 2048), np.float32)
    q = np.arange(256)
    for plane in range(2):
        for kt in range(4):
            for p in range(128):
                i_rel = kt * 128 + p - 256
                valid = (q >= i_rel) & (q <= i_rel + 256)
                if plane == 0 and first_is_start and (kt * 128 + p) < 256:
                    valid = np.zeros(256, bool)
                out[p, plane * 1024 + kt * 256:plane * 1024 + (kt + 1) * 256] = \
                    np.where(valid, 0.0, NEG_BIG)
    return out


def host_vtm(v):
    n = v.shape[0]
    out = np.zeros((n, HEADS * 65), np.float32)
    for h in range(HEADS):
        out[:, h * 65:h * 65 + 64] = v[:, h * 64:(h + 1) * 64]
        out[:, h * 65 + 64] = 1.0
    return out


NPAIR = 22
CH = 256
NCHK = NT // CH


def _build_FC(ms, act="gelu", rms=False, eps=1e-5, shift=True):
    ph = Phase("FC" + act)
    nc = ph.nc
    C = ph.C
    AFact = AF.Gelu if act == "gelu" else AF.Silu

    x2 = ph.inp("x2", (1 + NT, D))
    w1r = ph.inp("W1r", (D, NPAIR * 256))
    b1r = ph.inp("b1r", (128, 2 * NPAIR))
    w2r = ph.inp("W2r", (NPAIR * 128, D))
    b2v = ph.inp("b2", (D,))
    lng = ph.inp("lng", (D,))
    lnb = ph.inp("lnb", (D,))
    identb = ph.inp("identb", (128, 128), BF16)

    xi3 = ph.out("xi3", (1 + NT, D))
    xi3b = ph.out("xi3_bf", (1 + NT, D), BF16)

    NB = 2
    w1sb = nc.alloc_sbuf_tensor("w1sb", [128, 8 * NPAIR * 256], BF16)
    w2sb = nc.alloc_sbuf_tensor("w2sb", [128, NPAIR * 1024], BF16)
    b1sb = nc.alloc_sbuf_tensor("b1sb", [128, 2 * NPAIR], F32)
    gb = nc.alloc_sbuf_tensor("gb", [128, D], F32)
    bb = nc.alloc_sbuf_tensor("bb", [128, D], F32)
    b2b = nc.alloc_sbuf_tensor("b2b", [128, D], F32)
    idb = nc.alloc_sbuf_tensor("idb", [128, 128], BF16)
    xc = nc.alloc_sbuf_tensor("xc", [128, NB * 2 * 1024], F32)
    xu = nc.alloc_sbuf_tensor("xu", [128, NB * 2 * 512], F32)
    sq = nc.alloc_sbuf_tensor("sq", [128, 1024], F32)
    stats = nc.alloc_sbuf_tensor("stats", [128, 16], F32)
    xn0 = nc.alloc_sbuf_tensor("xn0", [128, 1024], F32)
    xnb = nc.alloc_sbuf_tensor("xnb", [128, 2 * 1024], BF16)
    xnfm = nc.alloc_sbuf_tensor("xnfm", [128, 2 * 1024], BF16)
    gg = nc.alloc_sbuf_tensor("gg", [128, 256], F32)
    prodb = nc.alloc_sbuf_tensor("prodb", [128, 2 * 256], BF16)
    tw = nc.alloc_sbuf_tensor("tw", [128, 1024], F32)
    xo = nc.alloc_sbuf_tensor("xo", [128, 2 * 1024], F32)
    xob = nc.alloc_sbuf_tensor("xob", [128, 2 * 1024], BF16)

    ptr = nc.alloc_psum_tensor("ptr", [128, 1024], BF16)
    p01 = nc.alloc_psum_tensor("p01", [128, 1024], F32)
    pw2 = nc.alloc_psum_tensor("pw2", [128, 2048], F32)

    with (
        nc.semaphore("s_ld0") as s_ld0,
        nc.semaphore("s_lda") as s_lda,
        nc.semaphore("s_ldb") as s_ldb,
        nc.semaphore("s_w") as s_w,
        nc.semaphore("s_mm") as s_mm,
        nc.semaphore("s_st") as s_st,
        nc.semaphore("chv") as chv,
        nc.semaphore("chs") as chs,
        nc.Block() as block,
    ):
        pe_n = [0]

        def pe_ms(mi, name):
            mi.then_inc(s_mm, 1)
            pe_n[0] += 1
            ms["t:" + name] = pe_n[0]

        @block.sync
        def _(sync):
            for c in range(NCHK):
                b = c % NB
                if c >= NB:
                    sync.wait_ge(s_mm, ms.get(f"t:w2_{c - NB}", 0))
                    sync.wait_ge(chv, ms.get(f"v:ev_{c - NB}", 0))
                    sync.wait_ge(chs, ms.get(f"s:cast_{c - NB}", 0))
                sld = s_lda if b == 0 else s_ldb
                for tt in range(2):
                    r0 = c * 256 + tt * 128
                    rsh = r0 if shift else (1 + r0)
                    sync.dma_start(out=xc[:, (b * 2 + tt) * 1024:(b * 2 + tt) * 1024 + 512],
                                   in_=ap(x2, [[D, 128], [1, 512]], (1 + r0) * D)
                                   ).then_inc(sld, 16)
                    sync.dma_start(out=xc[:, (b * 2 + tt) * 1024 + 512:(b * 2 + tt + 1) * 1024],
                                   in_=ap(x2, [[D, 128], [1, 512]], rsh * D + 512)
                                   ).then_inc(sld, 16)
                    sync.dma_start(out=xu[:, (b * 2 + tt) * 512:(b * 2 + tt + 1) * 512],
                                   in_=ap(x2, [[D, 128], [1, 512]], (1 + r0) * D + 512)
                                   ).then_inc(sld, 16)

        @block.gpsimd
        def _(g):
            ph.load_consts(g, s_ld0)
            g.dma_start(out=idb[:, :], in_=identb[:, :]).then_inc(s_ld0, 16)
            g.dma_start(out=b1sb[:, :], in_=b1r[:, :]).then_inc(s_ld0, 16)
            g.dma_start(out=gb[:, :], in_=ap(lng, [[0, 128], [1, D]])).then_inc(s_ld0, 16)
            g.dma_start(out=bb[:, :], in_=ap(lnb, [[0, 128], [1, D]])).then_inc(s_ld0, 16)
            g.dma_start(out=b2b[:, :], in_=ap(b2v, [[0, 128], [1, D]])).then_inc(s_ld0, 16)
            for k in range(8):
                g.dma_start(out=w1sb[:, k * NPAIR * 256:(k + 1) * NPAIR * 256],
                            in_=ap(w1r, [[NPAIR * 256, 128], [1, NPAIR * 256]],
                                   k * 128 * NPAIR * 256)).then_inc(s_w, 16)
            for k in range(NPAIR):
                g.dma_start(out=w2sb[:, k * 1024:(k + 1) * 1024],
                            in_=ap(w2r, [[D, 128], [1, D]], k * 128 * D)).then_inc(s_w, 16)
            for c in range(NCHK):
                g.wait_ge(chv, ms.get(f"v:ev_{c}", 0))
                g.wait_ge(chs, ms.get(f"s:cast_{c}", 0))
                for tt in range(2):
                    r0 = c * 256 + tt * 128
                    g.dma_start(out=ap(xi3, [[D, 128], [1, D]], (1 + r0) * D),
                                in_=xo[:, tt * 1024:(tt + 1) * 1024]).then_inc(s_st, 16)
                    g.dma_start(out=ap(xi3b, [[D, 128], [1, D]], (1 + r0) * D),
                                in_=xob[:, tt * 1024:(tt + 1) * 1024]).then_inc(s_st, 16)
            g.wait_ge(s_st, 64 * NCHK)

        @block.vector
        def _(v0):
            v = Ch(v0, chv, ms, "v")
            v.wait_ge(s_ld0, 16 * 6)
            for c in range(NCHK):
                b = c % NB
                v.wait_ge(s_lda if b == 0 else s_ldb, 6 * 16 * (c // NB + 1))
                for tt in range(2):
                    xct = xc[:, (b * 2 + tt) * 1024:(b * 2 + tt + 1) * 1024]
                    if not rms:
                        v.reduce_sum(stats[:, 0:1], xct, axis=AX.X)
                    v.wait_ge(chs, ms.get(f"s:sq_{c}_{tt}", 0))
                    v.reduce_sum(stats[:, 1:2], sq[:, :], axis=AX.X)
                    v.tensor_scalar(stats[:, 4:5], stats[:, 1:2], C(1.0 / D), None,
                                    ALU.mult)
                    if not rms:
                        v.tensor_scalar(stats[:, 2:3], stats[:, 0:1], C(1.0 / D), None,
                                        ALU.mult)
                        v.tensor_tensor(stats[:, 3:4], stats[:, 2:3], stats[:, 2:3],
                                        ALU.mult)
                        v.tensor_tensor(stats[:, 5:6], stats[:, 4:5], stats[:, 3:4],
                                        ALU.subtract)
                    else:
                        v.tensor_copy(stats[:, 5:6], stats[:, 4:5])
                    v.ms(f"var_{c}_{tt}")
                    v.wait_ge(chs, ms.get(f"s:sd_{c}_{tt}", 0))
                    v.reciprocal(stats[:, 7:8], stats[:, 6:7])
                    if not rms:
                        v.tensor_scalar(stats[:, 9:10], stats[:, 0:1], C(-1.0 / D),
                                        None, ALU.mult)
                        v.tensor_scalar(xn0[:, :], xct, stats[:, 9:10], stats[:, 7:8],
                                        ALU.add, ALU.mult)
                        v.tensor_tensor(xn0[:, :], xn0[:, :], gb[:, :], ALU.mult)
                        v.tensor_tensor(xnb[:, tt * 1024:(tt + 1) * 1024], xn0[:, :],
                                        bb[:, :], ALU.add)
                    else:
                        v.tensor_scalar(xn0[:, :], xct, stats[:, 7:8], None, ALU.mult)
                        v.tensor_tensor(xnb[:, tt * 1024:(tt + 1) * 1024], xn0[:, :],
                                        gb[:, :], ALU.mult)
                    v.ms(f"xnb_{c}_{tt}")
                for j in range(NPAIR):
                    jb = j % 2
                    v.wait_ge(s_mm, ms.get(f"t:a_{c}_{j}", 0))
                    v.wait_ge(chs, ms.get(f"s:gg_{c}_{j}", 0))
                    if j >= 2:
                        v.wait_ge(s_mm, ms.get(f"t:w2_{c}_{j - 2}", 0))
                    v.scalar_tensor_tensor(prodb[:, jb * 256:(jb + 1) * 256],
                                           p01[:, 512:768], b1sb[:, 2 * j:2 * j + 1],
                                           gg[:, 0:256], ALU.add, ALU.mult)
                    v.ms(f"prod_{c}_{j}")
                v.wait_ge(s_mm, ms.get(f"t:w2_{c}", 0))
                if c >= 1:
                    v.wait_ge(s_st, 64 * c)
                for tt in range(2):
                    v.tensor_tensor(tw[:, :], pw2[:, tt * 1024:(tt + 1) * 1024],
                                    b2b[:, :], ALU.add)
                    v.tensor_tensor(xo[:, tt * 1024:tt * 1024 + 512],
                                    tw[:, 0:512],
                                    xc[:, (b * 2 + tt) * 1024:(b * 2 + tt) * 1024 + 512],
                                    ALU.add)
                    v.tensor_tensor(xo[:, tt * 1024 + 512:(tt + 1) * 1024],
                                    tw[:, 512:1024],
                                    xu[:, (b * 2 + tt) * 512:(b * 2 + tt + 1) * 512],
                                    ALU.add)
                v.ms(f"ev_{c}")

        @block.scalar
        def _(sc0):
            sc = Ch(sc0, chs, ms, "s")
            sc.wait_ge(s_ld0, 16 * 6)
            for c in range(NCHK):
                b = c % NB
                sc.wait_ge(s_lda if b == 0 else s_ldb, 6 * 16 * (c // NB + 1))
                for tt in range(2):
                    xct = xc[:, (b * 2 + tt) * 1024:(b * 2 + tt + 1) * 1024]
                    sc.activation(sq[:, :], xct, AF.Square)
                    sc.ms(f"sq_{c}_{tt}")
                    sc.wait_ge(chv, ms.get(f"v:var_{c}_{tt}", 0))
                    sc.activation(stats[:, 6:7], stats[:, 5:6], AF.Sqrt, bias=C(eps))
                    sc.ms(f"sd_{c}_{tt}")
                    sc.wait_ge(s_mm, ms.get(f"t:tr_{c}_{tt}", 0))
                    sc.copy(xnfm[:, tt * 1024:(tt + 1) * 1024], ptr[:, :])
                    sc.ms(f"xnfm_{c}_{tt}")
                for j in range(NPAIR):
                    sc.wait_ge(s_mm, ms.get(f"t:g_{c}_{j}", 0))
                    sc.activation(gg[:, :], p01[:, 0:256], AFact,
                                  bias=b1sb[:, 2 * j + 1:2 * j + 2])
                    sc.ms(f"gg_{c}_{j}")
                sc.wait_ge(chv, ms.get(f"v:ev_{c}", 0))
                for tt in range(2):
                    sc.copy(xob[:, tt * 1024:(tt + 1) * 1024],
                            xo[:, tt * 1024:(tt + 1) * 1024])
                sc.ms(f"cast_{c}")

        @block.tensor
        def _(t):
            t.wait_ge(s_w, 16 * (8 + NPAIR))
            t.wait_ge(s_ld0, 16 * 6)
            for c in range(NCHK):
                b = c % NB
                for tt in range(2):
                    t.wait_ge(chv, ms.get(f"v:xnb_{c}_{tt}", 0))
                    if c > 0 or tt > 0:
                        t.wait_ge(chs, ms.get(f"s:xnfm_{c}_{tt - 1}" if tt
                                              else f"s:xnfm_{c - 1}_1", 0))
                    mi = None
                    for blk in range(8):
                        mi = t.transpose(ptr[:, blk * 128:(blk + 1) * 128],
                                         xnb[:, tt * 1024 + blk * 128:
                                             tt * 1024 + (blk + 1) * 128],
                                         idb[:, :])
                    pe_ms(mi, f"tr_{c}_{tt}")
                t.wait_ge(chs, ms.get(f"s:xnfm_{c}_1", 0))
                for j in range(NPAIR):
                    if j > 0:
                        t.wait_ge(chs, ms.get(f"s:gg_{c}_{j - 1}", 0))
                    elif c > 0:
                        t.wait_ge(chs, ms.get(f"s:gg_{c - 1}_{NPAIR - 1}", 0))
                    mi = None
                    for k in range(8):
                        mi = t.matmul(p01[:, 0:256],
                                      w1sb[:, k * NPAIR * 256 + j * 256 + 128:
                                           k * NPAIR * 256 + j * 256 + 256],
                                      xnfm[:, k * 128:(k + 1) * 128].rearrange(
                                          "p x -> p x") if False else
                                      ap(xnfm, [[2048, 128], [1024, 2], [1, 128]],
                                         k * 128),
                                      start=(k == 0), stop=(k == 7))
                    pe_ms(mi, f"g_{c}_{j}")
                    if j > 0:
                        t.wait_ge(chv, ms.get(f"v:prod_{c}_{j - 1}", 0))
                    elif c > 0:
                        t.wait_ge(chv, ms.get(f"v:prod_{c - 1}_{NPAIR - 1}", 0))
                    mi = None
                    for k in range(8):
                        mi = t.matmul(p01[:, 512:768],
                                      w1sb[:, k * NPAIR * 256 + j * 256:
                                           k * NPAIR * 256 + j * 256 + 128],
                                      ap(xnfm, [[2048, 128], [1024, 2], [1, 128]],
                                         k * 128),
                                      start=(k == 0), stop=(k == 7))
                    pe_ms(mi, f"a_{c}_{j}")
                    t.wait_ge(chv, ms.get(f"v:prod_{c}_{j}", 0))
                    if j == 0 and c >= 1:
                        t.wait_ge(chv, ms.get(f"v:ev_{c - 1}", 0))
                    mi = None
                    for tt in range(2):
                        for n in range(2):
                            mi = t.matmul(pw2[:, tt * 1024 + n * 512:
                                              tt * 1024 + (n + 1) * 512],
                                          prodb[:, (j % 2) * 256 + tt * 128:
                                                (j % 2) * 256 + (tt + 1) * 128],
                                          w2sb[:, j * 1024 + n * 512:
                                               j * 1024 + (n + 1) * 512],
                                          start=(j == 0), stop=(j == NPAIR - 1))
                    pe_ms(mi, f"w2_{c}_{j}")
                ms[f"t:w2_{c}"] = pe_n[0]

    return ph


def build_FC(**kw):
    return build_twice(lambda ms: _build_FC(ms, **kw))


def host_w1r(W1a, W1g, b1a, b1g):
    Dm = W1a.shape[0]
    w1r = np.zeros((Dm, NPAIR * 256), np.float32)
    b1r = np.zeros((128, 2 * NPAIR), np.float32)
    na = W1a.shape[1]
    for j in range(NPAIR):
        lo, hi = j * 128, min((j + 1) * 128, na)
        if lo < na:
            w1r[:, j * 256:j * 256 + (hi - lo)] = W1a[:, lo:hi]
            w1r[:, j * 256 + 128:j * 256 + 128 + (hi - lo)] = W1g[:, lo:hi]
            b1r[0:hi - lo, 2 * j] = b1a[lo:hi]
            b1r[0:hi - lo, 2 * j + 1] = b1g[lo:hi]
    return w1r, b1r


def host_w2r(W2):
    out = np.zeros((NPAIR * 128, W2.shape[1]), np.float32)
    out[:W2.shape[0]] = W2
    return out


GDK, GDV, GH = 128, 256, 4


def _build_G1(ms):
    ph = Phase("G1")
    nc = ph.nc
    C = ph.C

    xo_d = ph.inp("xo", (NT, D))
    wq_d = ph.inp("Wq", (D, 512))
    wk_d = ph.inp("Wk", (D, 512))
    wv_d = ph.inp("Wv", (D, 1024))
    wg_d = ph.inp("Wg", (D, 1024))
    wgk1_d = ph.inp("Wgk1", (D, 16))
    wgk2_d = ph.inp("Wgk2", (16, 512))
    bgk2_d = ph.inp("bgk2", (512,))
    anw_d = ph.inp("attn_norm_w", (D,))
    identb_d = ph.inp("identb", (128, 128), BF16)
    cumA_d = ph.inp("cumA_t", (128, 128))
    cumB_d = ph.inp("cumB_t", (128, 128))
    triu_d = ph.inp("triu01", (128, 128))
    ones16_d = ph.inp("ones16", (128, 1))

    u_o = ph.out("u", (NT, D))
    oraw_o = ph.out("o_raw", (NT, 1024))
    sg_o = ph.out("sg", (NT, 1024))
    qed_o = ph.out("qefm", (512, NT), BF16)
    brun_o = ph.out("brun", (NTILE, 128, 4))
    sout_o = ph.out("S_out", (128, 1024))

    NB = 2
    wq = nc.alloc_sbuf_tensor("wq", [128, 8 * 512], BF16)
    wk = nc.alloc_sbuf_tensor("wk", [128, 8 * 512], BF16)
    wv = nc.alloc_sbuf_tensor("wv", [128, 8 * 1024], BF16)
    wg = nc.alloc_sbuf_tensor("wg", [128, 8 * 1024], BF16)
    wgk1 = nc.alloc_sbuf_tensor("wgk1", [128, 8 * 16], BF16)
    wgk2 = nc.alloc_sbuf_tensor("wgk2", [16, 512], BF16)
    bgk2b = nc.alloc_sbuf_tensor("bgk2b", [128, 512], F32)
    anwb = nc.alloc_sbuf_tensor("anwb", [128, D], F32)
    idb = nc.alloc_sbuf_tensor("idb", [128, 128], BF16)
    cumA = nc.alloc_sbuf_tensor("cumA_s", [128, 128], F32)
    cumB = nc.alloc_sbuf_tensor("cumB_s", [128, 128], F32)
    triu01 = nc.alloc_sbuf_tensor("triu01_s", [128, 128], F32)
    ones16 = nc.alloc_sbuf_tensor("ones16_s", [128, 1], F32)

    xot = nc.alloc_sbuf_tensor("xot", [128, NB * 1024], F32)
    ut = nc.alloc_sbuf_tensor("ut", [128, 1024], F32)
    hf = nc.alloc_sbuf_tensor("hf", [128, 1024], F32)
    sq = nc.alloc_sbuf_tensor("sq", [128, 1024], F32)
    stats = nc.alloc_sbuf_tensor("stats", [128, 16], F32)
    hb = nc.alloc_sbuf_tensor("hb", [128, 1024], BF16)
    hfm = nc.alloc_sbuf_tensor("hfm", [128, 8 * 128], BF16)
    gk1b = nc.alloc_sbuf_tensor("gk1b", [16, 128], BF16)
    qf = nc.alloc_sbuf_tensor("qf", [128, 512], F32)
    kf = nc.alloc_sbuf_tensor("kf", [128, 512], F32)
    xpre = nc.alloc_sbuf_tensor("xpre", [128, 512], F32)
    gabs = nc.alloc_sbuf_tensor("gabs", [128, 512], F32)
    graw = nc.alloc_sbuf_tensor("graw", [128, 512], F32)
    eb = nc.alloc_sbuf_tensor("eb", [128, 512], F32)
    emb = nc.alloc_sbuf_tensor("emb", [128, 512], F32)
    erc = nc.alloc_sbuf_tensor("erc", [128, 512], F32)
    qe = nc.alloc_sbuf_tensor("qe", [128, 512], BF16)
    ke = nc.alloc_sbuf_tensor("ke", [128, 512], BF16)
    kdb = nc.alloc_sbuf_tensor("kdb", [128, 512], BF16)
    vb = nc.alloc_sbuf_tensor("vb", [128, 1024], BF16)
    sgb = nc.alloc_sbuf_tensor("sgb", [128, 1024], F32)
    qefm = nc.alloc_sbuf_tensor("qefm_s", [128, 4 * 128], BF16)
    kefm = nc.alloc_sbuf_tensor("kefm", [128, 4 * 128], BF16)
    ab = nc.alloc_sbuf_tensor("ab", [128, 512], BF16)
    of32 = nc.alloc_sbuf_tensor("of32", [128, 1024], F32)
    eblb = nc.alloc_sbuf_tensor("eblb", [128, 4], F32)
    brun = nc.alloc_sbuf_tensor("brun_s", [128, 4], F32)
    brun_pre = nc.alloc_sbuf_tensor("brun_pre", [128, 4], F32)
    S = nc.alloc_sbuf_tensor("S", [128, 1024], F32)
    Sb = nc.alloc_sbuf_tensor("Sb", [128, 1024], BF16)

    ptr = nc.alloc_psum_tensor("ptr", [128, 1024], BF16)
    pq = nc.alloc_psum_tensor("pq", [128, 512], F32)
    pk = nc.alloc_psum_tensor("pk", [128, 512], F32)
    pv = nc.alloc_psum_tensor("pv", [128, 1024], F32)
    pg = nc.alloc_psum_tensor("pg", [128, 1024], F32)
    pgk = nc.alloc_psum_tensor("pgk", [128, 512], F32)

    with (
        nc.semaphore("s_ld0") as s_ld0,
        nc.semaphore("s_lda") as s_lda,
        nc.semaphore("s_ldb") as s_ldb,
        nc.semaphore("s_w") as s_w,
        nc.semaphore("s_mm") as s_mm,
        nc.semaphore("st_u") as st_u,
        nc.semaphore("st_sg") as st_sg,
        nc.semaphore("st_qe") as st_qe,
        nc.semaphore("st_br") as st_br,
        nc.semaphore("st_o") as st_o,
        nc.semaphore("chv") as chv,
        nc.semaphore("chs") as chs,
        nc.Block() as block,
    ):
        pe_n = [0]

        def pe_ms(mi, name):
            mi.then_inc(s_mm, 1)
            pe_n[0] += 1
            ms["t:" + name] = pe_n[0]

        @block.sync
        def _(sync):
            for i in range(NTILE):
                b = i % NB
                if i >= NB:
                    sync.wait_ge(chs, ms.get(f"s:sq_{i - NB + 1}", 0))
                (sync.dma_start(out=xot[:, b * 1024:(b + 1) * 1024],
                                in_=ap(xo_d, [[D, 128], [1, D]], i * 128 * D))
                 .then_inc(s_lda if b == 0 else s_ldb, 16))

        @block.gpsimd
        def _(g):
            ph.load_consts(g, s_ld0)
            for tsb, d_ in ((idb, identb_d), (cumA, cumA_d), (cumB, cumB_d),
                            (triu01, triu_d), (ones16, ones16_d)):
                g.dma_start(out=tsb[:, :], in_=d_[:, :]).then_inc(s_ld0, 16)
            g.dma_start(out=bgk2b[:, :], in_=ap(bgk2_d, [[0, 128], [1, 512]])
                        ).then_inc(s_ld0, 16)
            g.dma_start(out=anwb[:, :], in_=ap(anw_d, [[0, 128], [1, D]])
                        ).then_inc(s_ld0, 16)
            for k in range(8):
                g.dma_start(out=wq[:, k * 512:(k + 1) * 512],
                            in_=ap(wq_d, [[512, 128], [1, 512]], k * 128 * 512)
                            ).then_inc(s_w, 16)
                g.dma_start(out=wk[:, k * 512:(k + 1) * 512],
                            in_=ap(wk_d, [[512, 128], [1, 512]], k * 128 * 512)
                            ).then_inc(s_w, 16)
                g.dma_start(out=wv[:, k * 1024:(k + 1) * 1024],
                            in_=ap(wv_d, [[1024, 128], [1, 1024]], k * 128 * 1024)
                            ).then_inc(s_w, 16)
                g.dma_start(out=wg[:, k * 1024:(k + 1) * 1024],
                            in_=ap(wg_d, [[1024, 128], [1, 1024]], k * 128 * 1024)
                            ).then_inc(s_w, 16)
                g.dma_start(out=wgk1[:, k * 16:(k + 1) * 16],
                            in_=ap(wgk1_d, [[16, 128], [1, 16]], k * 128 * 16)
                            ).then_inc(s_w, 16)
            g.dma_start(out=wgk2[:, :], in_=wgk2_d[:, :]).then_inc(s_w, 16)
            for i in range(NTILE):
                g.wait_ge(chv, ms.get(f"v:udone_{i}", 0))
                g.dma_start(out=ap(u_o, [[D, 128], [1, D]], i * 128 * D),
                            in_=ut[:, :]).then_inc(st_u, 16)
                g.wait_ge(chs, ms.get(f"s:sg_{i}", 0))
                g.dma_start(out=ap(sg_o, [[1024, 128], [1, 1024]], i * 128 * 1024),
                            in_=sgb[:, :]).then_inc(st_sg, 16)
                g.wait_ge(chs, ms.get(f"s:qek_{i}", 0))
                for h in range(4):
                    g.dma_start(out=ap(qed_o, [[NT, 128], [1, 128]],
                                       h * 128 * NT + i * 128),
                                in_=qefm[:, h * 128:(h + 1) * 128]).then_inc(st_qe, 16)
                g.wait_ge(chv, ms.get(f"v:brun_{i}", 0))
                g.dma_start(out=ap(brun_o, [[4, 128], [1, 4]], i * 512),
                            in_=brun_pre[:, :]).then_inc(st_br, 16)
                g.wait_ge(chs, ms.get(f"s:oev_{i}", 0))
                g.dma_start(out=ap(oraw_o, [[1024, 128], [1, 1024]], i * 128 * 1024),
                            in_=of32[:, :]).then_inc(st_o, 16)
            g.wait_ge(chv, ms.get("v:sfinal", 0))
            g.dma_start(out=sout_o[:, :], in_=S[:, :]).then_inc(st_u, 16)
            g.wait_ge(st_u, 16 * (NTILE + 1))
            g.wait_ge(st_sg, 16 * NTILE)
            g.wait_ge(st_qe, 64 * NTILE)
            g.wait_ge(st_br, 16 * NTILE)
            g.wait_ge(st_o, 16 * NTILE)

        @block.vector
        def _(v0):
            v = Ch(v0, chv, ms, "v")
            v.memset(S[:, :], 0.0)
            v.memset(Sb[:, :], 0.0)
            v.memset(brun[:, :], 0.0)
            v.wait_ge(s_ld0, 16 * 8)
            for i in range(NTILE):
                b = i % NB
                v.wait_ge(s_lda if b == 0 else s_ldb, 16 * (i // NB + 1))
                xt = xot[:, b * 1024:(b + 1) * 1024]
                v.reduce_sum(stats[:, 0:1], xt, axis=AX.X)
                v.wait_ge(chs, ms.get(f"s:sq_{i}", 0))
                v.reduce_sum(stats[:, 1:2], sq[:, :], axis=AX.X)
                v.tensor_scalar(stats[:, 4:5], stats[:, 1:2], C(1.0 / D), None, ALU.mult)
                v.tensor_scalar(stats[:, 2:3], stats[:, 0:1], C(1.0 / D), None, ALU.mult)
                v.tensor_tensor(stats[:, 3:4], stats[:, 2:3], stats[:, 2:3], ALU.mult)
                v.tensor_tensor(stats[:, 5:6], stats[:, 4:5], stats[:, 3:4], ALU.subtract)
                v.ms(f"var_{i}")
                v.wait_ge(chs, ms.get(f"s:sd_{i}", 0))
                v.reciprocal(stats[:, 7:8], stats[:, 6:7])
                v.tensor_scalar(stats[:, 9:10], stats[:, 0:1], C(-1.0 / D), None, ALU.mult)
                if i > 0:
                    v.wait_ge(st_u, 16 * i)
                v.tensor_scalar(ut[:, :], xt, stats[:, 9:10], stats[:, 7:8],
                                ALU.add, ALU.mult)
                v.ms(f"udone_{i}")
                v.wait_ge(chs, ms.get(f"s:sq2_{i}", 0))
                v.reduce_sum(stats[:, 1:2], sq[:, :], axis=AX.X)
                v.tensor_scalar(stats[:, 5:6], stats[:, 1:2], C(1.0 / D), None, ALU.mult)
                v.ms(f"var2_{i}")
                v.wait_ge(chs, ms.get(f"s:sd2_{i}", 0))
                v.reciprocal(stats[:, 7:8], stats[:, 6:7])
                if i > 0:
                    v.wait_ge(s_mm, ms.get(f"t:trh_{i - 1}", 0))
                v.tensor_scalar(hf[:, :], ut[:, :], stats[:, 7:8], None, ALU.mult)
                v.tensor_tensor(hf[:, :], hf[:, :], anwb[:, :], ALU.mult)
                v.tensor_copy(hb[:, :], hf[:, :])
                v.ms(f"hb_{i}")
                v.wait_ge(s_mm, ms.get(f"t:glog_{i}", 0))
                v.tensor_tensor(xpre[:, :], pq[:, :], bgk2b[:, :], ALU.add)
                v.ms(f"xpre_{i}")
                v.wait_ge(chs, ms.get(f"s:lse_{i}", 0))
                if i > 0:
                    v.wait_ge(s_mm, ms.get(f"t:blast_{i - 1}", 0))
                v.tensor_scalar(graw[:, :], xpre[:, :], C(0.0), None, ALU.min)
                v.tensor_tensor(graw[:, :], graw[:, :], gabs[:, :], ALU.subtract)
                v.ms(f"graw_{i}")
                v.wait_ge(chs, ms.get(f"s:exps_{i}", 0))
                if i > 0:
                    v.wait_ge(s_mm, ms.get(f"t:ds_{i - 1}", 0))
                v.scalar_tensor_tensor(qe[:, :], qf[:, :], C(0.08838834764831845),
                                       eb[:, :], ALU.mult, ALU.mult)
                v.tensor_tensor(ke[:, :], kf[:, :], emb[:, :], ALU.mult)
                v.tensor_tensor(kdb[:, :], kf[:, :], erc[:, :], ALU.mult)
                v.ms(f"qeke_{i}")
                v.wait_ge(s_mm, ms.get(f"t:blast_{i}", 0))
                if i > 0:
                    v.wait_ge(st_br, 16 * i)
                v.tensor_copy(brun_pre[:, :], brun[:, :])
                v.tensor_tensor(brun[:, :], brun[:, :], pgk[:, 0:4], ALU.add)
                v.ms(f"brun_{i}")
                v.wait_ge(s_mm, ms.get(f"t:at_{i}", 0))
                v.tensor_tensor(ab[:, :], pq[:, :],
                                ap(triu01, [[128, 128], [0, 4], [1, 128]]), ALU.mult)
                v.ms(f"ab_{i}")
                v.wait_ge(s_mm, ms.get(f"t:ds_{i}", 0))
                v.wait_ge(chs, ms.get(f"s:ebl_{i}", 0))
                v.tensor_tensor(S[:, :], S[:, :],
                                ap(eblb, [[4, 128], [1, 4], [0, 256]]), ALU.mult)
                v.tensor_tensor(S[:, :], S[:, :], pg[:, :], ALU.add)
                v.ms(f"supd_{i}")
            v.ms("sfinal")

        @block.scalar
        def _(sc0):
            sc = Ch(sc0, chs, ms, "s")
            sc.wait_ge(s_ld0, 16 * 8)
            for i in range(NTILE):
                b = i % NB
                sc.wait_ge(s_lda if b == 0 else s_ldb, 16 * (i // NB + 1))
                if i > 0:
                    sc.wait_ge(chv, ms.get(f"v:hb_{i - 1}", 0))
                xt = xot[:, b * 1024:(b + 1) * 1024]
                sc.activation(sq[:, :], xt, AF.Square)
                sc.ms(f"sq_{i}")
                sc.wait_ge(chv, ms.get(f"v:var_{i}", 0))
                sc.activation(stats[:, 6:7], stats[:, 5:6], AF.Sqrt, bias=C(1e-5))
                sc.ms(f"sd_{i}")
                sc.wait_ge(chv, ms.get(f"v:udone_{i}", 0))
                sc.activation(sq[:, :], ut[:, :], AF.Square)
                sc.ms(f"sq2_{i}")
                sc.wait_ge(chv, ms.get(f"v:var2_{i}", 0))
                sc.activation(stats[:, 6:7], stats[:, 5:6], AF.Sqrt, bias=C(1e-6))
                sc.ms(f"sd2_{i}")
                sc.wait_ge(s_mm, ms.get(f"t:trh_{i}", 0))
                sc.copy(hfm[:, :], ptr[:, 0:1024])
                sc.ms(f"hfm_{i}")
                sc.wait_ge(s_mm, ms.get(f"t:pq_{i}", 0))
                sc.copy(qf[:, :], pq[:, :])
                sc.ms(f"qf_{i}")
                sc.wait_ge(s_mm, ms.get(f"t:pk_{i}", 0))
                sc.copy(kf[:, :], pk[:, :])
                sc.ms(f"kf_{i}")
                sc.wait_ge(s_mm, ms.get(f"t:pv_{i}", 0))
                if i > 0:
                    sc.wait_ge(s_mm, ms.get(f"t:ds_{i - 1}", 0))
                sc.copy(vb[:, :], pv[:, :])
                sc.ms(f"vb_{i}")
                sc.wait_ge(s_mm, ms.get(f"t:pg_{i}", 0))
                if i > 0:
                    sc.wait_ge(st_sg, 16 * i)
                sc.activation(sgb[:, :], pg[:, :], AF.Silu)
                sc.ms(f"sg_{i}")
                sc.wait_ge(s_mm, ms.get(f"t:gk1_{i}", 0))
                sc.copy(gk1b[:, :], pgk[0:16, 0:128])
                sc.ms(f"gk1b_{i}")
                sc.wait_ge(chv, ms.get(f"v:xpre_{i}", 0))
                if i > 0:
                    sc.wait_ge(chv, ms.get(f"v:graw_{i - 1}", 0))
                sc.activation(gabs[:, :], xpre[:, :], AF.Abs)
                sc.activation(gabs[:, :], gabs[:, :], AF.Exp, scale=C(-1.0))
                sc.activation(gabs[:, :], gabs[:, :], AF.Ln, bias=C(1.0))
                sc.ms(f"lse_{i}")
                sc.wait_ge(s_mm, ms.get(f"t:cum_{i}", 0))
                if i > 0:
                    sc.wait_ge(chv, ms.get(f"v:qeke_{i - 1}", 0))
                sc.activation(eb[:, :], pk[:, :], AF.Exp)
                sc.activation(emb[:, :], pk[:, :], AF.Exp, scale=C(-1.0))
                sc.activation(erc[:, :], pgk[:, :], AF.Exp)
                sc.ms(f"exps_{i}")
                sc.wait_ge(s_mm, ms.get(f"t:blast_{i}", 0))
                sc.activation(eblb[:, :], pgk[:, 0:4], AF.Exp)
                sc.ms(f"ebl_{i}")
                sc.wait_ge(s_mm, ms.get(f"t:trq_{i}", 0))
                if i > 0:
                    sc.wait_ge(st_qe, 64 * i)
                sc.copy(qefm[:, :], ptr[:, 0:512])
                sc.copy(kefm[:, :], ptr[:, 512:1024])
                sc.ms(f"qek_{i}")
                sc.wait_ge(s_mm, ms.get(f"t:o_{i}", 0))
                if i > 0:
                    sc.wait_ge(st_o, 16 * i)
                sc.copy(of32[:, :], pv[:, :])
                sc.ms(f"oev_{i}")
                sc.wait_ge(chv, ms.get(f"v:supd_{i}", 0))
                sc.copy(Sb[:, :], S[:, :])
                sc.ms(f"scast_{i}")

        @block.tensor
        def _(t):
            t.wait_ge(s_w, 16 * 41)
            t.wait_ge(s_ld0, 16 * 8)
            for i in range(NTILE):
                t.wait_ge(chv, ms.get(f"v:hb_{i}", 0))
                if i > 0:
                    t.wait_ge(chs, ms.get(f"s:qek_{i - 1}", 0))
                mi = None
                for blk in range(8):
                    mi = t.transpose(ptr[:, blk * 128:(blk + 1) * 128],
                                     hb[:, blk * 128:(blk + 1) * 128], idb[:, :])
                pe_ms(mi, f"trh_{i}")
                t.wait_ge(chs, ms.get(f"s:hfm_{i}", 0))
                if i > 0:
                    t.wait_ge(chv, ms.get(f"v:ab_{i - 1}", 0))
                mi = None
                for k in range(8):
                    mi = t.matmul(pq[:, :], hfm[:, k * 128:(k + 1) * 128],
                                  wq[:, k * 512:(k + 1) * 512],
                                  start=(k == 0), stop=(k == 7))
                pe_ms(mi, f"pq_{i}")
                if i > 0:
                    t.wait_ge(chs, ms.get(f"s:exps_{i - 1}", 0))
                mi = None
                for k in range(8):
                    mi = t.matmul(pk[:, :], hfm[:, k * 128:(k + 1) * 128],
                                  wk[:, k * 512:(k + 1) * 512],
                                  start=(k == 0), stop=(k == 7))
                pe_ms(mi, f"pk_{i}")
                if i > 0:
                    t.wait_ge(chs, ms.get(f"s:oev_{i - 1}", 0))
                mi = None
                for n in range(2):
                    for k in range(8):
                        mi = t.matmul(pv[:, n * 512:(n + 1) * 512],
                                      hfm[:, k * 128:(k + 1) * 128],
                                      wv[:, k * 1024 + n * 512:k * 1024 + (n + 1) * 512],
                                      start=(k == 0), stop=(k == 7))
                pe_ms(mi, f"pv_{i}")
                if i > 0:
                    t.wait_ge(chv, ms.get(f"v:supd_{i - 1}", 0))
                mi = None
                for n in range(2):
                    for k in range(8):
                        mi = t.matmul(pg[:, n * 512:(n + 1) * 512],
                                      hfm[:, k * 128:(k + 1) * 128],
                                      wg[:, k * 1024 + n * 512:k * 1024 + (n + 1) * 512],
                                      start=(k == 0), stop=(k == 7))
                pe_ms(mi, f"pg_{i}")
                if i > 0:
                    t.wait_ge(chs, ms.get(f"s:ebl_{i - 1}", 0))
                    t.wait_ge(chv, ms.get(f"v:brun_{i - 1}", 0))
                mi = None
                for k in range(8):
                    mi = t.matmul(pgk[0:16, 0:128], wgk1[:, k * 16:(k + 1) * 16],
                                  hfm[:, k * 128:(k + 1) * 128],
                                  start=(k == 0), stop=(k == 7))
                pe_ms(mi, f"gk1_{i}")
                t.wait_ge(chs, ms.get(f"s:qf_{i}", 0))
                t.wait_ge(chs, ms.get(f"s:gk1b_{i}", 0))
                mi = t.matmul(pq[:, :], gk1b[:, :], wgk2[:, :], start=True, stop=True)
                pe_ms(mi, f"glog_{i}")
                t.wait_ge(chv, ms.get(f"v:graw_{i}", 0))
                t.wait_ge(chs, ms.get(f"s:kf_{i}", 0))
                mi = t.matmul(pk[:, :], cumA[:, :], graw[:, :], start=True, stop=True)
                pe_ms(mi, f"cumb_{i}")
                mi = t.matmul(pgk[:, :], cumB[:, :], graw[:, :], start=True, stop=True)
                pe_ms(mi, f"cum_{i}")
                t.wait_ge(chs, ms.get(f"s:exps_{i}", 0))
                mi = None
                for h in range(4):
                    mi = t.matmul(pgk[:, h:h + 1], graw[:, h * 128:(h + 1) * 128],
                                  ones16[:, :], start=True, stop=True)
                pe_ms(mi, f"blast_{i}")
                t.wait_ge(chv, ms.get(f"v:qeke_{i}", 0))
                mi = None
                for blk in range(4):
                    mi = t.transpose(ptr[:, blk * 128:(blk + 1) * 128],
                                     qe[:, blk * 128:(blk + 1) * 128], idb[:, :])
                for blk in range(4):
                    mi = t.transpose(ptr[:, 512 + blk * 128:512 + (blk + 1) * 128],
                                     ke[:, blk * 128:(blk + 1) * 128], idb[:, :])
                pe_ms(mi, f"trq_{i}")
                t.wait_ge(chv, ms.get(f"v:xpre_{i}", 0))
                t.wait_ge(chs, ms.get(f"s:qek_{i}", 0))
                mi = None
                for h in range(4):
                    mi = t.matmul(pq[:, h * 128:(h + 1) * 128],
                                  kefm[:, h * 128:(h + 1) * 128],
                                  qefm[:, h * 128:(h + 1) * 128],
                                  start=True, stop=True)
                pe_ms(mi, f"at_{i}")
                t.wait_ge(chv, ms.get(f"v:ab_{i}", 0))
                t.wait_ge(chs, ms.get(f"s:vb_{i}", 0))
                if i > 0:
                    t.wait_ge(chs, ms.get(f"s:scast_{i - 1}", 0))
                mi = None
                for h in range(4):
                    mi = t.matmul(pv[:, h * 256:(h + 1) * 256],
                                  ab[:, h * 128:(h + 1) * 128],
                                  vb[:, h * 256:(h + 1) * 256],
                                  start=True, stop=False)
                    mi = t.matmul(pv[:, h * 256:(h + 1) * 256],
                                  qefm[:, h * 128:(h + 1) * 128],
                                  Sb[:, h * 256:(h + 1) * 256],
                                  start=False, stop=True)
                pe_ms(mi, f"o_{i}")
                t.wait_ge(chs, ms.get(f"s:sg_{i}", 0))
                mi = None
                for h in range(4):
                    mi = t.matmul(pg[:, h * 256:(h + 1) * 256],
                                  kdb[:, h * 128:(h + 1) * 128],
                                  vb[:, h * 256:(h + 1) * 256],
                                  start=True, stop=True)
                pe_ms(mi, f"ds_{i}")

    return ph


def build_G1():
    return build_twice(_build_G1)


def host_g1_consts():
    s = np.arange(128)[:, None]
    t = np.arange(128)[None, :]
    return {
        "cumA_t": ((s <= t) / 16.0).astype(np.float32),
        "cumB_t": ((s > t) / 16.0).astype(np.float32),
        "triu01": (s <= t).astype(np.float32),
        "ones16": np.full((128, 1), 1.0 / 16.0, np.float32),
    }


def _build_G2(ms):
    ph = Phase("G2")
    nc = ph.nc
    C = ph.C

    u_d = ph.inp("u", (NT, D))
    oraw_d = ph.inp("o_raw", (NT, 1024))
    sg_d = ph.inp("sg", (NT, 1024))
    qed_d = ph.inp("qefm", (512, NT), BF16)
    brun_d = ph.inp("brun", (NTILE, 128, 4))
    sin_d = ph.inp("S_in", (128, 1024))
    gnw_d = ph.inp("gnorm_w", (256,))
    wo_d = ph.inp("Wo", (1024, D))
    identb_d = ph.inp("identb", (128, 128), BF16)

    x2_o = ph.out("x2", (1 + NT, D))

    NB = 2
    wo = nc.alloc_sbuf_tensor("wo", [128, 8 * 1024], BF16)
    gnwb = nc.alloc_sbuf_tensor("gnwb", [128, 1024], F32)
    idb = nc.alloc_sbuf_tensor("idb", [128, 128], BF16)
    sinb = nc.alloc_sbuf_tensor("sinb", [128, 1024], BF16)
    ut = nc.alloc_sbuf_tensor("ut", [128, NB * 1024], F32)
    ot = nc.alloc_sbuf_tensor("ot", [128, NB * 1024], F32)
    sgt = nc.alloc_sbuf_tensor("sgt", [128, NB * 1024], F32)
    qet = nc.alloc_sbuf_tensor("qet", [128, NB * 512], BF16)
    brt = nc.alloc_sbuf_tensor("brt", [128, NB * 4], F32)
    ebr = nc.alloc_sbuf_tensor("ebr", [128, 4], F32)
    qe2 = nc.alloc_sbuf_tensor("qe2", [128, 512], BF16)
    sq = nc.alloc_sbuf_tensor("sq", [128, 1024], F32)
    rr = nc.alloc_sbuf_tensor("rr", [128, 16], F32)
    onb = nc.alloc_sbuf_tensor("onb", [128, 1024], BF16)
    ofm = nc.alloc_sbuf_tensor("ofm", [128, 1024], BF16)
    x2sb = nc.alloc_sbuf_tensor("x2sb", [128, 1024], F32)

    pc = nc.alloc_psum_tensor("pc", [128, 1024], F32)
    ptr = nc.alloc_psum_tensor("ptr", [128, 1024], BF16)
    pwo = nc.alloc_psum_tensor("pwo", [128, 1024], F32)

    with (
        nc.semaphore("s_ld0") as s_ld0,
        nc.semaphore("s_lda") as s_lda,
        nc.semaphore("s_ldb") as s_ldb,
        nc.semaphore("s_w") as s_w,
        nc.semaphore("s_mm") as s_mm,
        nc.semaphore("s_st") as s_st,
        nc.semaphore("chv") as chv,
        nc.semaphore("chs") as chs,
        nc.Block() as block,
    ):
        pe_n = [0]

        def pe_ms(mi, name):
            mi.then_inc(s_mm, 1)
            pe_n[0] += 1
            ms["t:" + name] = pe_n[0]

        @block.sync
        def _(sync):
            for i in range(NTILE):
                b = i % NB
                if i >= NB:
                    sync.wait_ge(chv, ms.get(f"v:x2_{i - NB}", 0))
                sld = s_lda if b == 0 else s_ldb
                sync.dma_start(out=ut[:, b * 1024:(b + 1) * 1024],
                               in_=ap(u_d, [[D, 128], [1, D]], i * 128 * D)
                               ).then_inc(sld, 16)
                sync.dma_start(out=ot[:, b * 1024:(b + 1) * 1024],
                               in_=ap(oraw_d, [[1024, 128], [1, 1024]], i * 128 * 1024)
                               ).then_inc(sld, 16)
                sync.dma_start(out=sgt[:, b * 1024:(b + 1) * 1024],
                               in_=ap(sg_d, [[1024, 128], [1, 1024]], i * 128 * 1024)
                               ).then_inc(sld, 16)
                for h in range(4):
                    sync.dma_start(out=qet[:, b * 512 + h * 128:b * 512 + (h + 1) * 128],
                                   in_=ap(qed_d, [[NT, 128], [1, 128]],
                                          h * 128 * NT + i * 128)).then_inc(sld, 16)
                sync.dma_start(out=brt[:, b * 4:(b + 1) * 4],
                               in_=ap(brun_d, [[4, 128], [1, 4]], i * 512)
                               ).then_inc(sld, 16)

        @block.gpsimd
        def _(g):
            ph.load_consts(g, s_ld0)
            g.dma_start(out=idb[:, :], in_=identb_d[:, :]).then_inc(s_ld0, 16)
            g.dma_start(out=gnwb[:, :],
                        in_=ap(gnw_d, [[0, 128], [0, 4], [1, 256]])).then_inc(s_ld0, 16)
            g.dma_start(out=sinb[:, :], in_=sin_d[:, :]).then_inc(s_ld0, 16)
            for k in range(8):
                g.dma_start(out=wo[:, k * 1024:(k + 1) * 1024],
                            in_=ap(wo_d, [[D, 128], [1, D]], k * 128 * D)
                            ).then_inc(s_w, 16)
            for i in range(NTILE):
                g.wait_ge(chv, ms.get(f"v:x2_{i}", 0))
                g.dma_start(out=ap(x2_o, [[D, 128], [1, D]], (1 + i * 128) * D),
                            in_=x2sb[:, :]).then_inc(s_st, 16)
            g.wait_ge(s_st, 16 * NTILE)

        @block.scalar
        def _(sc0):
            sc = Ch(sc0, chs, ms, "s")
            sc.wait_ge(s_ld0, 16 * 4)
            for i in range(NTILE):
                b = i % NB
                sc.wait_ge(s_lda if b == 0 else s_ldb, 16 * 8 * (i // NB + 1))
                if i > 0:
                    sc.wait_ge(s_mm, ms.get(f"t:corr_{i - 1}", 0))
                sc.activation(ebr[:, :], brt[:, b * 4:(b + 1) * 4], AF.Exp)
                sc.ms(f"ebr_{i}")
                sc.wait_ge(chv, ms.get(f"v:osum_{i}", 0))
                sc.activation(rr[:, 4:8], rr[:, 0:4], AF.Sqrt, bias=C(1e-6))
                sc.ms(f"sd_{i}")
                sc.wait_ge(s_mm, ms.get(f"t:tro_{i}", 0))
                sc.copy(ofm[:, :], ptr[:, :])
                sc.ms(f"ofm_{i}")

        @block.vector
        def _(v0):
            v = Ch(v0, chv, ms, "v")
            v.wait_ge(s_ld0, 16 * 4)
            for i in range(NTILE):
                b = i % NB
                v.wait_ge(s_lda if b == 0 else s_ldb, 16 * 8 * (i // NB + 1))
                v.wait_ge(chs, ms.get(f"s:ebr_{i}", 0))
                if i > 0:
                    v.wait_ge(s_mm, ms.get(f"t:corr_{i - 1}", 0))
                for h in range(4):
                    v.tensor_tensor(qe2[:, h * 128:(h + 1) * 128],
                                    qet[:, b * 512 + h * 128:b * 512 + (h + 1) * 128],
                                    ap(ebr, [[4, 128], [0, 128]], h), ALU.mult)
                v.ms(f"qe2_{i}")
                v.wait_ge(s_mm, ms.get(f"t:corr_{i}", 0))
                v.tensor_tensor(ot[:, b * 1024:(b + 1) * 1024],
                                ot[:, b * 1024:(b + 1) * 1024], pc[:, :], ALU.add)
                v.tensor_tensor(sq[:, :], ot[:, b * 1024:(b + 1) * 1024],
                                ot[:, b * 1024:(b + 1) * 1024], ALU.mult)
                v.reduce_sum(rr[:, 0:4], ap(sq, [[1024, 128], [256, 4], [1, 256]]),
                             axis=AX.X)
                v.tensor_scalar(rr[:, 0:4], rr[:, 0:4], C(1.0 / 256.0), None, ALU.mult)
                v.ms(f"osum_{i}")
                v.wait_ge(chs, ms.get(f"s:sd_{i}", 0))
                v.reciprocal(rr[:, 8:12], rr[:, 4:8])
                v.tensor_tensor(ot[:, b * 1024:(b + 1) * 1024],
                                ot[:, b * 1024:(b + 1) * 1024],
                                ap(rr, [[16, 128], [1, 4], [0, 256]], 8), ALU.mult)
                v.tensor_tensor(ot[:, b * 1024:(b + 1) * 1024],
                                ot[:, b * 1024:(b + 1) * 1024], gnwb[:, :], ALU.mult)
                if i > 0:
                    v.wait_ge(s_mm, ms.get(f"t:tro_{i - 1}", 0))
                v.tensor_tensor(onb[:, :], ot[:, b * 1024:(b + 1) * 1024],
                                sgt[:, b * 1024:(b + 1) * 1024], ALU.mult)
                v.ms(f"on_{i}")
                v.wait_ge(s_mm, ms.get(f"t:wo_{i}", 0))
                if i > 0:
                    v.wait_ge(s_st, 16 * i)
                v.tensor_tensor(x2sb[:, :], pwo[:, :],
                                ut[:, b * 1024:(b + 1) * 1024], ALU.add)
                v.ms(f"x2_{i}")

        @block.tensor
        def _(t):
            t.wait_ge(s_w, 16 * 8)
            t.wait_ge(s_ld0, 16 * 4)
            for i in range(NTILE):
                t.wait_ge(chv, ms.get(f"v:qe2_{i}", 0))
                if i > 0:
                    t.wait_ge(chv, ms.get(f"v:osum_{i - 1}", 0))
                mi = None
                for h in range(4):
                    mi = t.matmul(pc[:, h * 256:(h + 1) * 256],
                                  qe2[:, h * 128:(h + 1) * 128],
                                  sinb[:, h * 256:(h + 1) * 256],
                                  start=True, stop=True)
                pe_ms(mi, f"corr_{i}")
                t.wait_ge(chv, ms.get(f"v:on_{i}", 0))
                if i > 0:
                    t.wait_ge(chs, ms.get(f"s:ofm_{i - 1}", 0))
                mi = None
                for blk in range(8):
                    mi = t.transpose(ptr[:, blk * 128:(blk + 1) * 128],
                                     onb[:, blk * 128:(blk + 1) * 128], idb[:, :])
                pe_ms(mi, f"tro_{i}")
                t.wait_ge(chs, ms.get(f"s:ofm_{i}", 0))
                if i > 0:
                    t.wait_ge(chv, ms.get(f"v:x2_{i - 1}", 0))
                mi = None
                for n in range(2):
                    for k in range(8):
                        mi = t.matmul(pwo[:, n * 512:(n + 1) * 512],
                                      ofm[:, k * 128:(k + 1) * 128],
                                      wo[:, k * 1024 + n * 512:k * 1024 + (n + 1) * 512],
                                      start=(k == 0), stop=(k == 7))
                pe_ms(mi, f"wo_{i}")

    return ph


def build_G2():
    return build_twice(_build_G2)


MD_IN = 2048
MCONV = 2304
MNH = 32
MPROJ = 4384


def _build_M1(ms):
    ph = Phase("M1")
    nc = ph.nc
    C = ph.C

    ubf_d = ph.inp("u_bf", (3 + NT, D), BF16)
    winT_d = ph.inp("W_inT", (D, MPROJ))
    cwT_d = ph.inp("conv_wT", (4, MCONV))
    cb_d = ph.inp("conv_b", (MCONV,))
    dtb_d = ph.inp("dt_bias", (MNH,))
    al_d = ph.inp("A_log", (MNH,))
    dd_d = ph.inp("D_h", (MNH,))
    tril_d = ph.inp("triu01", (128, 128))
    trils_d = ph.inp("trils", (128, 128))
    madd_d = ph.inp("maddT", (128, 128))
    onesr_d = ph.inp("onesrow", (1, 128))
    onesc_d = ph.inp("onescol", (128, 1))
    identb_d = ph.inp("identb", (128, 128), BF16)
    identf_d = ph.inp("identf", (128, 128))

    sz_o = ph.out("sz", (NT, MD_IN))
    yraw_o = ph.out("y_raw", (NT, MD_IN))
    cumt_o = ph.out("cumt", (NT, MNH))
    crun_o = ph.out("crun", (NTILE, MNH))
    cfm_o = ph.out("Cfm", (128, NT), BF16)
    sout_o = ph.out("S_out", (128, MD_IN))

    xbct = nc.dram_tensor("xbct", [3 + NT, MCONV], F32)
    ctf = nc.dram_tensor("ctf", [MNH, 128], F32)

    winT = nc.alloc_sbuf_tensor("winT", [128, 8 * MPROJ], BF16)
    wcb = nc.alloc_sbuf_tensor("wcb", [128, 4 * MCONV], BF16)
    cbb = nc.alloc_sbuf_tensor("cbb", [128, MCONV], F32)
    dtbb = nc.alloc_sbuf_tensor("dtbb", [128, MNH], F32)
    negAb = nc.alloc_sbuf_tensor("negAb", [128, MNH], F32)
    Db = nc.alloc_sbuf_tensor("Db", [128, MNH], F32)
    tril01 = nc.alloc_sbuf_tensor("tril01_s", [128, 128], F32)
    trils = nc.alloc_sbuf_tensor("trils_s", [128, 128], F32)
    maddT = nc.alloc_sbuf_tensor("maddT_s", [128, 128], F32)
    onesr = nc.alloc_sbuf_tensor("onesr_s", [1, 128], F32)
    onesc = nc.alloc_sbuf_tensor("onesc_s", [128, 1], F32)
    idb = nc.alloc_sbuf_tensor("idb", [128, 128], BF16)
    idf = nc.alloc_sbuf_tensor("idf", [128, 128], F32)

    ufm = nc.alloc_sbuf_tensor("ufm", [128, 2 * 8 * 128], BF16)
    uh3 = nc.alloc_sbuf_tensor("uh3", [128, 8 * 16], BF16)
    hx = nc.alloc_sbuf_tensor("hx", [16, 512], F32)
    xbcs = nc.alloc_sbuf_tensor("xbcs", [128, MCONV], F32)
    szs = nc.alloc_sbuf_tensor("szs", [128, 512], F32)
    xcv = nc.alloc_sbuf_tensor("xcv", [128, 2 * MCONV], F32)
    tj = nc.alloc_sbuf_tensor("tj", [128, MD_IN], F32)
    xcs = nc.alloc_sbuf_tensor("xcs", [128, MCONV], F32)
    dtraw = nc.alloc_sbuf_tensor("dtraw", [128, MNH], F32)
    dts = nc.alloc_sbuf_tensor("dts", [128, MNH], F32)
    dta = nc.alloc_sbuf_tensor("dta", [128, MNH], F32)
    gab = nc.alloc_sbuf_tensor("gab", [128, MNH], F32)
    cums = nc.alloc_sbuf_tensor("cums", [128, MNH], F32)
    ecum = nc.alloc_sbuf_tensor("ecum", [128, MNH], F32)
    erc = nc.alloc_sbuf_tensor("erc", [128, MNH], F32)
    ctos = nc.alloc_sbuf_tensor("ctos", [1, MNH], F32)
    ectos = nc.alloc_sbuf_tensor("ectos", [1, MNH], F32)
    cracc = nc.alloc_sbuf_tensor("cracc", [1, MNH], F32)
    crpre = nc.alloc_sbuf_tensor("crpre", [1, MNH], F32)
    cumf = nc.alloc_sbuf_tensor("cumf", [32, 128], F32)
    mdd = nc.alloc_sbuf_tensor("mdd", [128, 16 * 128], F32)
    mtb = nc.alloc_sbuf_tensor("mtb", [128, 2 * 16 * 128], BF16)
    dtxb = nc.alloc_sbuf_tensor("dtxb", [128, MD_IN], BF16)
    dtxd = nc.alloc_sbuf_tensor("dtxd", [128, MD_IN], BF16)
    Bb = nc.alloc_sbuf_tensor("Bb", [128, 128], BF16)
    Cb = nc.alloc_sbuf_tensor("Cb", [128, 128], BF16)
    Bfm = nc.alloc_sbuf_tensor("Bfm", [128, 128], BF16)
    Cfm = nc.alloc_sbuf_tensor("Cfm_s", [128, 128], BF16)
    yint = nc.alloc_sbuf_tensor("yint", [128, MD_IN], F32)
    yrs = nc.alloc_sbuf_tensor("yrs", [128, MD_IN], F32)
    S = nc.alloc_sbuf_tensor("S", [128, MD_IN], F32)
    Sb = nc.alloc_sbuf_tensor("Sb", [128, MD_IN], BF16)

    p1 = nc.alloc_psum_tensor("p1", [128, 512], F32)
    pmA = nc.alloc_psum_tensor("pmA", [128, 512], F32)
    pmB = nc.alloc_psum_tensor("pmB", [128, 512], F32)
    ptrb = nc.alloc_psum_tensor("ptrb", [128, 512], BF16)
    pyr = nc.alloc_psum_tensor("pyr", [128, 2048], F32)

    with (
        nc.semaphore("s_ld0") as s_ld0,
        nc.semaphore("s_lh") as s_lh,
        nc.semaphore("s_lda") as s_lda,
        nc.semaphore("s_ldb") as s_ldb,
        nc.semaphore("s_cva") as s_cva,
        nc.semaphore("s_cvb") as s_cvb,
        nc.semaphore("s_w") as s_w,
        nc.semaphore("s_mm") as s_mm,
        nc.semaphore("st_xb") as st_xb,
        nc.semaphore("st_xh") as st_xh,
        nc.semaphore("st_sz") as st_sz,
        nc.semaphore("st_y") as st_y,
        nc.semaphore("st_cu") as st_cu,
        nc.semaphore("st_cr") as st_cr,
        nc.semaphore("st_cf") as st_cf,
        nc.semaphore("st_ct") as st_ct,
        nc.semaphore("s_md") as s_md,
        nc.semaphore("chv") as chv,
        nc.semaphore("chs") as chs,
        nc.Block() as block,
    ):
        pe_n = [0]

        def pe_ms(mi, name):
            mi.then_inc(s_mm, 1)
            pe_n[0] += 1
            ms["t:" + name] = pe_n[0]

        @block.sync
        def _(sync):
            for blk in range(8):
                sync.dma_start(out=ap(uh3, [[128, 128], [1, 16]], blk * 16),
                               in_=ap(ubf_d, [[D, 16], [1, 128]], blk * 128),
                               transpose=True).then_inc(s_lh, 16)
            for i in range(NTILE):
                b = i % 2
                if i >= 2:
                    sync.wait_ge(s_mm, ms.get(f"t:proj_{i - 2}", 0))
                sld = s_lda if b == 0 else s_ldb
                for blk in range(8):
                    sync.dma_start(
                        out=ap(ufm, [[2 * 8 * 128, 128], [1, 128]],
                               b * 1024 + blk * 128),
                        in_=ap(ubf_d, [[D, 128], [1, 128]],
                               (3 + i * 128) * D + blk * 128),
                        transpose=True).then_inc(sld, 16)

        @block.gpsimd
        def _(g):
            ph.load_consts(g, s_ld0)
            for tsb, d_ in ((tril01, tril_d), (trils, trils_d), (maddT, madd_d),
                            (idb, identb_d), (idf, identf_d)):
                g.dma_start(out=tsb[:, :], in_=d_[:, :]).then_inc(s_ld0, 16)
            g.dma_start(out=onesr[:, :], in_=onesr_d[:, :]).then_inc(s_ld0, 16)
            g.dma_start(out=onesc[:, :], in_=onesc_d[:, :]).then_inc(s_ld0, 16)
            for j in range(4):
                g.dma_start(out=wcb[:, j * MCONV:(j + 1) * MCONV],
                            in_=ap(cwT_d, [[0, 128], [1, MCONV]], j * MCONV)
                            ).then_inc(s_ld0, 16)
            g.dma_start(out=cbb[:, :], in_=ap(cb_d, [[0, 128], [1, MCONV]])
                        ).then_inc(s_ld0, 16)
            g.dma_start(out=dtbb[:, :], in_=ap(dtb_d, [[0, 128], [1, MNH]])
                        ).then_inc(s_ld0, 16)
            g.dma_start(out=negAb[:, :], in_=ap(al_d, [[0, 128], [1, MNH]])
                        ).then_inc(s_ld0, 16)
            g.dma_start(out=Db[:, :], in_=ap(dd_d, [[0, 128], [1, MNH]])
                        ).then_inc(s_ld0, 16)
            for k in range(8):
                g.dma_start(out=winT[:, k * MPROJ:(k + 1) * MPROJ],
                            in_=ap(winT_d, [[MPROJ, 128], [1, MPROJ]], k * 128 * MPROJ)
                            ).then_inc(s_w, 16)
            for n in range(5):
                wn = min(512, MCONV - n * 512)
                g.wait_ge(chs, ms.get(f"s:hx_{n}", 0))
                g.dma_start(out=ap(xbct, [[MCONV, 3], [1, wn]], n * 512),
                            in_=ap(hx, [[512, 3], [1, wn]])).then_inc(st_xh, 16)
            for i in range(NTILE):
                b = i % 2
                for n in range(4):
                    g.wait_ge(chs, ms.get(f"s:sz_{i}_{n}", 0))
                    g.dma_start(out=ap(sz_o, [[MD_IN, 128], [1, 512]],
                                       i * 128 * MD_IN + n * 512),
                                in_=szs[:, :]).then_inc(st_sz, 16)
                g.wait_ge(chs, ms.get(f"s:xbcs_{i}", 0))
                g.dma_start(out=ap(xbct, [[MCONV, 128], [1, MCONV]],
                                   (3 + i * 128) * MCONV),
                            in_=xbcs[:, :]).then_inc(st_xb, 16)
                g.wait_ge(st_xh, 16 * 5)
                g.wait_ge(st_xb, 16 * (i + 1))
                if i >= 2:
                    g.wait_ge(chs, ms.get(f"s:xconv_{i - 2}", 0))
                scv = s_cva if b == 0 else s_cvb
                for j in range(2):
                    g.dma_start(out=xcv[:, j * MCONV:(j + 1) * MCONV],
                                in_=ap(xbct, [[MCONV, 128], [1, MCONV]],
                                       (i * 128 + j) * MCONV)).then_inc(scv, 16)
                g.wait_ge(chv, ms.get(f"v:cv01_{i}", 0))
                for j in range(2, 4):
                    g.dma_start(out=xcv[:, (j - 2) * MCONV:(j - 1) * MCONV],
                                in_=ap(xbct, [[MCONV, 128], [1, MCONV]],
                                       (i * 128 + j) * MCONV)).then_inc(scv, 16)
                g.wait_ge(chs, ms.get(f"s:cume_{i}", 0))
                g.dma_start(out=ap(cumt_o, [[MNH, 128], [1, MNH]], i * 128 * MNH),
                            in_=cums[:, :]).then_inc(st_cu, 16)
                g.wait_ge(chv, ms.get(f"v:crun_{i}", 0))
                g.dma_start(out=ap(crun_o, [[MNH, 1], [1, MNH]], i * MNH),
                            in_=crpre[:, :]).then_inc(st_cr, 16)
                g.wait_ge(chs, ms.get(f"s:cumf_{i}", 0))
                g.dma_start(out=ctf[:, :], in_=cumf[:, :]).then_inc(st_ct, 16)
                g.wait_ge(st_ct, 16 * (i + 1))
                for hh2 in range(2):
                    g.wait_ge(chv, ms.get(f"v:mtbh_{i}_{hh2 - 1}" if hh2 > 0
                                          else (f"v:mtbh_{i - 1}_1" if i > 0 else "zz"), 0))
                    g.dma_start(out=mdd[:, :],
                                in_=ap(ctf, [[0, 128], [1, 16 * 128]], hh2 * 16 * 128)
                                ).then_inc(s_md, 16)
                g.wait_ge(chs, ms.get(f"s:cfm_{i}", 0))
                g.dma_start(out=ap(cfm_o, [[NT, 128], [1, 128]], i * 128),
                            in_=Cfm[:, :]).then_inc(st_cf, 16)
                g.wait_ge(chv, ms.get(f"v:y_{i}", 0))
                g.dma_start(out=ap(yraw_o, [[MD_IN, 128], [1, MD_IN]], i * 128 * MD_IN),
                            in_=yrs[:, :]).then_inc(st_y, 16)
            g.wait_ge(chv, ms.get("v:sfinal", 0))
            g.dma_start(out=sout_o[:, :], in_=S[:, :]).then_inc(st_y, 16)
            g.wait_ge(st_y, 16 * (NTILE + 1))
            g.wait_ge(st_sz, 16 * 4 * NTILE)
            g.wait_ge(st_cu, 16 * NTILE)
            g.wait_ge(st_cr, 16 * NTILE)
            g.wait_ge(st_cf, 16 * NTILE)

        @block.tensor
        def _(t):
            t.wait_ge(s_w, 16 * 8)
            t.wait_ge(s_ld0, 16 * 16)
            t.wait_ge(s_lh, 16 * 8)
            for n in range(5):
                w0 = MD_IN + n * 512
                wn = min(512, MCONV - n * 512)
                if n > 0:
                    t.wait_ge(chs, ms.get(f"s:hx_{n - 1}", 0))
                mi = None
                for k in range(8):
                    mi = t.matmul(p1[0:16, 0:wn],
                                  ap(uh3, [[8 * 16, 128], [1, 16]], k * 16),
                                  ap(winT, [[8 * MPROJ, 128], [1, wn]],
                                     k * MPROJ + w0),
                                  start=(k == 0), stop=(k == 7))
                pe_ms(mi, f"h3_{n}")
            for i in range(NTILE):
                b = i % 2
                t.wait_ge(s_lda if b == 0 else s_ldb, 16 * 8 * (i // 2 + 1))
                if i == 0:
                    t.wait_ge(chs, ms.get("s:hx_4", 0))
                uv = lambda k: ap(ufm, [[2 * 8 * 128, 128], [1, 128]],
                                  b * 1024 + k * 128)
                for n in range(4):
                    if n > 0 or i > 0:
                        t.wait_ge(chs, ms.get(f"s:sz_{i}_{n - 1}" if n > 0
                                              else f"s:dt_{i - 1}", 0))
                    mi = None
                    for k in range(8):
                        mi = t.matmul(p1[:, :], uv(k),
                                      ap(winT, [[8 * MPROJ, 128], [1, 512]],
                                         k * MPROJ + n * 512),
                                      start=(k == 0), stop=(k == 7))
                    pe_ms(mi, f"z_{i}_{n}")
                for n in range(5):
                    w0 = MD_IN + n * 512
                    wn = min(512, MCONV - n * 512)
                    t.wait_ge(chs, ms.get(f"s:xbc_{i}_{n - 1}" if n > 0
                                          else f"s:sz_{i}_3", 0))
                    mi = None
                    for k in range(8):
                        mi = t.matmul(p1[:, 0:wn], uv(k),
                                      ap(winT, [[8 * MPROJ, 128], [1, wn]],
                                         k * MPROJ + w0),
                                      start=(k == 0), stop=(k == 7))
                    pe_ms(mi, f"xbc_{i}_{n}")
                t.wait_ge(chs, ms.get(f"s:xbc_{i}_4", 0))
                mi = None
                for k in range(8):
                    mi = t.matmul(p1[:, 0:MNH], uv(k),
                                  ap(winT, [[8 * MPROJ, 128], [1, MNH]],
                                     k * MPROJ + MD_IN + MCONV),
                                  start=(k == 0), stop=(k == 7))
                pe_ms(mi, f"proj_{i}")
                t.wait_ge(chv, ms.get(f"v:dta_{i}", 0))
                if i > 0:
                    t.wait_ge(chs, ms.get(f"s:cumf_{i - 1}", 0))
                mi = t.matmul(pmA[:, 0:MNH], tril01[:, :], dta[:, :],
                              start=True, stop=True)
                pe_ms(mi, f"cum_{i}")
                mi = t.matmul(pmA[:, 32:64], trils[:, :], dta[:, :],
                              start=True, stop=True)
                pe_ms(mi, f"rc_{i}")
                mi = t.matmul(pmA[0:1, 64:96], onesc[:, :], dta[:, :],
                              start=True, stop=True)
                pe_ms(mi, f"ctot_{i}")
                t.wait_ge(chs, ms.get(f"s:pmAev_{i}", 0))
                mi = t.transpose(ap(pmA, [[512, 32], [1, 128]], 256), cums[:, :],
                                 idf[:, :])
                pe_ms(mi, f"trcum_{i}")
                t.wait_ge(chv, ms.get(f"v:bc_{i}", 0))
                if i > 0:
                    t.wait_ge(chs, ms.get(f"s:bcf_{i - 1}", 0))
                mi = t.transpose(ptrb[:, 0:128], Bb[:, :], idb[:, :])
                mi = t.transpose(ptrb[:, 128:256], Cb[:, :], idb[:, :])
                pe_ms(mi, f"trbc_{i}")
                t.wait_ge(chs, ms.get(f"s:bcf_{i}", 0))
                if i > 0:
                    t.wait_ge(chv, ms.get(f"v:supd_{i - 1}", 0))
                mi = t.matmul(pmB[:, 0:128], Bfm[:, :], Cfm[:, :],
                              start=True, stop=True)
                pe_ms(mi, f"pbc_{i}")
                if i > 0:
                    t.wait_ge(chs, ms.get(f"s:scast_{i - 1}", 0))
                mi = None
                for n in range(4):
                    mi = t.matmul(pyr[:, n * 512:(n + 1) * 512], Cfm[:, :],
                                  Sb[:, n * 512:(n + 1) * 512], start=True, stop=True)
                pe_ms(mi, f"yr_{i}")
                t.wait_ge(chv, ms.get(f"v:yint_{i}", 0))
                t.wait_ge(chv, ms.get(f"v:mtb_{i}", 0))
                mi = None
                for h in range(MNH):
                    mi = t.matmul(pyr[:, h * 64:(h + 1) * 64],
                                  mtb[:, h * 128:(h + 1) * 128],
                                  dtxb[:, h * 64:(h + 1) * 64],
                                  start=True, stop=True)
                pe_ms(mi, f"yi_{i}")
                t.wait_ge(chv, ms.get(f"v:y_{i}", 0))
                mi = None
                for n in range(4):
                    mi = t.matmul(pyr[:, n * 512:(n + 1) * 512], Bb[:, :],
                                  dtxd[:, n * 512:(n + 1) * 512], start=True, stop=True)
                pe_ms(mi, f"ds_{i}")
                t.wait_ge(chs, ms.get(f"s:ecto_{i}", 0))
                t.wait_ge(chv, ms.get(f"v:mtb_{i}", 0))
                mi = t.matmul(pmB[:, 128:160], onesr[:, :], ectos[:, :],
                              start=True, stop=True)
                pe_ms(mi, f"pet_{i}")

        @block.vector
        def _(v0):
            v = Ch(v0, chv, ms, "v")
            v.memset(S[:, :], 0.0)
            v.memset(Sb[:, :], 0.0)
            v.memset(cracc[:, :], 0.0)
            v.wait_ge(s_ld0, 16 * 16)
            for i in range(NTILE):
                b = i % 2
                v.wait_ge(s_cva if b == 0 else s_cvb, 16 * (4 * (i // 2) + 2))
                if i > 0:
                    v.wait_ge(chs, ms.get(f"s:xconv_{i - 1}", 0))
                    v.wait_ge(chv, ms.get(f"v:bc_{i - 1}", 0))
                    v.wait_ge(s_mm, ms.get(f"t:ds_{i - 1}", 0))
                v.tensor_tensor(xcs[:, :], xcv[:, 0:MCONV],
                                ap(wcb, [[4 * MCONV, 128], [1, MCONV]], 0), ALU.mult)
                for hs in range(2):
                    o0, w0_ = hs * 1152, 1152
                    v.tensor_tensor(tj[:, 0:1152], xcv[:, MCONV + o0:MCONV + o0 + 1152],
                                    ap(wcb, [[4 * MCONV, 128], [1, 1152]], MCONV + o0),
                                    ALU.mult)
                    v.tensor_tensor(xcs[:, o0:o0 + 1152], xcs[:, o0:o0 + 1152],
                                    tj[:, 0:1152], ALU.add)
                v.ms(f"cv01_{i}")
                v.wait_ge(s_cva if b == 0 else s_cvb, 16 * (4 * (i // 2) + 4))
                for j in range(2, 4):
                    for hs in range(2):
                        o0 = hs * 1152
                        v.tensor_tensor(tj[:, 0:1152],
                                        xcv[:, (j - 2) * MCONV + o0:
                                            (j - 2) * MCONV + o0 + 1152],
                                        ap(wcb, [[4 * MCONV, 128], [1, 1152]],
                                           j * MCONV + o0), ALU.mult)
                        v.tensor_tensor(xcs[:, o0:o0 + 1152], xcs[:, o0:o0 + 1152],
                                        tj[:, 0:1152], ALU.add)
                v.tensor_tensor(xcs[:, :], xcs[:, :], cbb[:, :], ALU.add)
                v.ms(f"conv_{i}")
                v.wait_ge(chs, ms.get(f"s:dt_{i}", 0))
                v.tensor_tensor(dtraw[:, :], dtraw[:, :], dtbb[:, :], ALU.add)
                v.ms(f"dtb_{i}")
                v.wait_ge(chs, ms.get(f"s:lse_{i}", 0))
                v.tensor_scalar(dts[:, :], dtraw[:, :], C(0.0), None, ALU.max)
                v.tensor_tensor(dts[:, :], dts[:, :], gab[:, :], ALU.add)
                v.tensor_tensor(dta[:, :], dts[:, :], negAb[:, :], ALU.mult)
                v.ms(f"dta_{i}")
                v.wait_ge(chs, ms.get(f"s:cto_{i}", 0))
                if i > 0:
                    v.wait_ge(st_cr, 16 * i)
                v.tensor_copy(crpre[:, :], cracc[:, :])
                v.tensor_tensor(cracc[:, :], cracc[:, :], ctos[:, :], ALU.add)
                v.ms(f"crun_{i}")
                v.wait_ge(chs, ms.get(f"s:xconv_{i}", 0))
                v.wait_ge(chs, ms.get(f"s:exps_{i}", 0))
                if i > 0:
                    v.wait_ge(s_mm, ms.get(f"t:ds_{i - 1}", 0))
                v.tensor_tensor(dtxb[:, :], xcs[:, 0:MD_IN],
                                ap(dts, [[MNH, 128], [1, MNH], [0, 64]]), ALU.mult)
                v.tensor_tensor(dtxd[:, :], dtxb[:, :],
                                ap(erc, [[MNH, 128], [1, MNH], [0, 64]]), ALU.mult)
                v.wait_ge(s_mm, ms.get(f"t:trbc_{i - 1}", 0)) if i > 0 else None
                v.tensor_copy(Bb[:, :], xcs[:, MD_IN:MD_IN + 128])
                v.tensor_copy(Cb[:, :], xcs[:, MD_IN + 128:MD_IN + 256])
                v.ms(f"bc_{i}")
                for hh2 in range(2):
                    v.wait_ge(s_md, 16 * (2 * i + hh2 + 1))
                    v.tensor_tensor(mdd[:, :], mdd[:, :],
                                    ap(cums, [[MNH, 128], [1, 16], [0, 128]],
                                       hh2 * 16),
                                    ALU.subtract)
                    v.tensor_tensor(mdd[:, :], mdd[:, :],
                                    ap(maddT, [[128, 128], [0, 16], [1, 128]]),
                                    ALU.add)
                    v.ms(f"dtile_{i}_{hh2}")
                    v.wait_ge(chs, ms.get(f"s:texp_{i}_{hh2}", 0))
                    v.wait_ge(s_mm, ms.get(f"t:pbc_{i}", 0))
                    if i > 0 and hh2 == 0:
                        v.wait_ge(s_mm, ms.get(f"t:yi_{i - 1}", 0))
                    v.tensor_tensor(mtb[:, hh2 * 2048:(hh2 + 1) * 2048], mdd[:, :],
                                    ap(pmB, [[512, 128], [0, 16], [1, 128]]), ALU.mult)
                    v.ms(f"mtbh_{i}_{hh2}")
                v.ms(f"mtb_{i}")
                v.wait_ge(s_mm, ms.get(f"t:yr_{i}", 0))
                v.tensor_tensor(yint[:, :], pyr[:, :],
                                ap(ecum, [[MNH, 128], [1, MNH], [0, 64]]), ALU.mult)
                v.ms(f"yint_{i}")
                v.wait_ge(s_mm, ms.get(f"t:yi_{i}", 0))
                if i > 0:
                    v.wait_ge(st_y, 16 * i)
                v.tensor_tensor(yrs[:, :], pyr[:, :], yint[:, :], ALU.add)
                v.tensor_tensor(tj[:, :], xcs[:, 0:MD_IN],
                                ap(Db, [[MNH, 128], [1, MNH], [0, 64]]), ALU.mult)
                v.tensor_tensor(yrs[:, :], yrs[:, :], tj[:, :], ALU.add)
                v.ms(f"y_{i}")
                v.wait_ge(s_mm, ms.get(f"t:ds_{i}", 0))
                v.wait_ge(s_mm, ms.get(f"t:pet_{i}", 0))
                v.tensor_tensor(S[:, :], S[:, :],
                                ap(pmB, [[512, 128], [1, MNH], [0, 64]], 128), ALU.mult)
                v.tensor_tensor(S[:, :], S[:, :], pyr[:, :], ALU.add)
                v.ms(f"supd_{i}")
            v.ms("sfinal")

        @block.scalar
        def _(sc0):
            sc = Ch(sc0, chs, ms, "s")
            sc.wait_ge(s_ld0, 16 * 16)
            sc.activation(negAb[:, :], negAb[:, :], AF.Exp)
            sc.mul(negAb[:, :], negAb[:, :], C(-1.0))
            sc.ms("negA")
            for n in range(5):
                wn = min(512, MCONV - n * 512)
                sc.wait_ge(s_mm, ms.get(f"t:h3_{n}", 0))
                if n > 0:
                    sc.wait_ge(st_xh, 16 * n)
                sc.copy(hx[:, 0:wn], p1[0:16, 0:wn])
                sc.ms(f"hx_{n}")
            for i in range(NTILE):
                b = i % 2
                for n in range(4):
                    sc.wait_ge(s_mm, ms.get(f"t:z_{i}_{n}", 0))
                    nprev = 4 * i + n - 1
                    if nprev >= 0:
                        sc.wait_ge(st_sz, 16 * (nprev + 1))
                    sc.activation(szs[:, :], p1[:, :], AF.Silu)
                    sc.ms(f"sz_{i}_{n}")
                for n in range(5):
                    wn = min(512, MCONV - n * 512)
                    sc.wait_ge(s_mm, ms.get(f"t:xbc_{i}_{n}", 0))
                    if n == 0 and i > 0:
                        sc.wait_ge(st_xb, 16 * i)
                    sc.copy(xbcs[:, n * 512:n * 512 + wn], p1[:, 0:wn])
                    sc.ms(f"xbc_{i}_{n}")
                sc.ms(f"xbcs_{i}")
                sc.wait_ge(s_mm, ms.get(f"t:proj_{i}", 0))
                if i > 0:
                    sc.wait_ge(chv, ms.get(f"v:dta_{i - 1}", 0))
                sc.copy(dtraw[:, :], p1[:, 0:MNH])
                sc.ms(f"dt_{i}")
                sc.wait_ge(chv, ms.get(f"v:dtb_{i}", 0))
                sc.activation(gab[:, :], dtraw[:, :], AF.Abs)
                sc.activation(gab[:, :], gab[:, :], AF.Exp, scale=C(-1.0))
                sc.activation(gab[:, :], gab[:, :], AF.Ln, bias=C(1.0))
                sc.ms(f"lse_{i}")
                sc.wait_ge(chv, ms.get(f"v:conv_{i}", 0))
                sc.activation(xcs[:, :], xcs[:, :], AF.Silu)
                sc.ms(f"xconv_{i}")
                sc.wait_ge(s_mm, ms.get(f"t:ctot_{i}", 0))
                if i > 0:
                    sc.wait_ge(st_cu, 16 * i)
                    sc.wait_ge(chv, ms.get(f"v:dtile_{i}", 0) * 0 + 0) if False else None
                sc.copy(cums[:, :], pmA[:, 0:MNH])
                sc.activation(ecum[:, :], pmA[:, 0:MNH], AF.Exp)
                sc.activation(erc[:, :], pmA[:, 32:64], AF.Exp)
                sc.wait_ge(chv, ms.get(f"v:crun_{i - 1}", 0)) if i > 0 else None
                sc.copy(ctos[:, :], pmA[0:1, 64:96])
                sc.activation(ectos[:, :], ctos[:, :], AF.Exp)
                sc.ms(f"cto_{i}")
                sc.ms(f"ecto_{i}")
                sc.ms(f"cume_{i}")
                sc.ms(f"pmAev_{i}")
                sc.wait_ge(s_mm, ms.get(f"t:trcum_{i}", 0))
                if i > 0:
                    sc.wait_ge(st_ct, 16 * i)
                sc.copy(cumf[:, :], ap(pmA, [[512, 32], [1, 128]], 256))
                sc.ms(f"cumf_{i}")
                sc.wait_ge(s_mm, ms.get(f"t:trbc_{i}", 0))
                if i > 0:
                    sc.wait_ge(st_cf, 16 * i)
                    sc.wait_ge(s_mm, ms.get(f"t:pbc_{i - 1}", 0))
                sc.copy(Bfm[:, :], ptrb[:, 0:128])
                sc.copy(Cfm[:, :], ptrb[:, 128:256])
                sc.ms(f"bcf_{i}")
                sc.ms(f"cfm_{i}")
                for hh2 in range(2):
                    sc.wait_ge(chv, ms.get(f"v:dtile_{i}_{hh2}", 0))
                    sc.activation(mdd[:, :], mdd[:, :], AF.Exp)
                    sc.ms(f"texp_{i}_{hh2}")
                sc.wait_ge(chv, ms.get(f"v:supd_{i}", 0))
                sc.copy(Sb[:, :], S[:, :])
                sc.ms(f"scast_{i}")

    return ph


def build_M1():
    return build_twice(_build_M1)


def host_m1_consts():
    s = np.arange(128)[:, None]
    t = np.arange(128)[None, :]
    return {
        "triu01": (s <= t).astype(np.float32),
        "trils": (s > t).astype(np.float32),
        "maddT": np.where(s <= t, 0.0, NEG_BIG).astype(np.float32),
        "onesrow": np.ones((1, 128), np.float32),
        "onescol": np.ones((128, 1), np.float32),
    }


def _build_M2(ms):
    ph = Phase("M2")
    nc = ph.nc
    C = ph.C

    yraw_d = ph.inp("y_raw", (NT, MD_IN))
    sz_d = ph.inp("sz", (NT, MD_IN))
    cumt_d = ph.inp("cumt", (NT, MNH))
    crun_d = ph.inp("crun", (NTILE, MNH))
    cfm_d = ph.inp("Cfm", (128, NT), BF16)
    sin_d = ph.inp("S_in", (128, MD_IN))
    nw_d = ph.inp("norm_w", (MD_IN,))
    woT_d = ph.inp("W_outT", (MD_IN, D))
    identb_d = ph.inp("identb", (128, 128), BF16)
    onesr_d = ph.inp("onesrow", (1, 128))

    vout_o = ph.out("vout", (NT, D))

    NB = 2
    wosb = nc.alloc_sbuf_tensor("wosb", [128, 16 * 1024], BF16)
    nwb = nc.alloc_sbuf_tensor("nwb", [128, MD_IN], F32)
    idb = nc.alloc_sbuf_tensor("idb", [128, 128], BF16)
    onesr = nc.alloc_sbuf_tensor("onesr_s", [1, 128], F32)
    sinf = nc.alloc_sbuf_tensor("sinf", [128, MD_IN], F32)
    sinb = nc.alloc_sbuf_tensor("sinb", [128, MD_IN], BF16)
    yt = nc.alloc_sbuf_tensor("yt", [128, NB * MD_IN], F32)
    szt = nc.alloc_sbuf_tensor("szt", [128, NB * MD_IN], F32)
    cmt = nc.alloc_sbuf_tensor("cmt", [128, NB * MNH], F32)
    crn = nc.alloc_sbuf_tensor("crn", [1, NB * MNH], F32)
    cfb = nc.alloc_sbuf_tensor("cfb", [128, NB * 128], BF16)
    ecf = nc.alloc_sbuf_tensor("ecf", [128, MNH], F32)
    t2 = nc.alloc_sbuf_tensor("t2", [128, MD_IN], F32)
    sq = nc.alloc_sbuf_tensor("sq", [128, MD_IN], F32)
    stats = nc.alloc_sbuf_tensor("stats", [128, 8], F32)
    ynb = nc.alloc_sbuf_tensor("ynb", [128, MD_IN], BF16)
    yfm = nc.alloc_sbuf_tensor("yfm", [128, MD_IN], BF16)
    vo = nc.alloc_sbuf_tensor("vo", [128, 1024], F32)

    pc = nc.alloc_psum_tensor("pc", [128, 2048], F32)
    pm2 = nc.alloc_psum_tensor("pm2", [128, 512], F32)
    ptr = nc.alloc_psum_tensor("ptr", [128, 2048], BF16)

    with (
        nc.semaphore("s_ld0") as s_ld0,
        nc.semaphore("s_lda") as s_lda,
        nc.semaphore("s_ldb") as s_ldb,
        nc.semaphore("s_w") as s_w,
        nc.semaphore("s_mm") as s_mm,
        nc.semaphore("s_st") as s_st,
        nc.semaphore("chv") as chv,
        nc.semaphore("chs") as chs,
        nc.Block() as block,
    ):
        pe_n = [0]

        def pe_ms(mi, name):
            mi.then_inc(s_mm, 1)
            pe_n[0] += 1
            ms["t:" + name] = pe_n[0]

        @block.sync
        def _(sync):
            for i in range(NTILE):
                b = i % NB
                if i >= NB:
                    sync.wait_ge(chv, ms.get(f"v:y2_{i - NB}", 0))
                    sync.wait_ge(s_mm, ms.get(f"t:corr_{i - NB}", 0))
                sld = s_lda if b == 0 else s_ldb
                sync.dma_start(out=yt[:, b * MD_IN:(b + 1) * MD_IN],
                               in_=ap(yraw_d, [[MD_IN, 128], [1, MD_IN]],
                                      i * 128 * MD_IN)).then_inc(sld, 16)
                sync.dma_start(out=szt[:, b * MD_IN:(b + 1) * MD_IN],
                               in_=ap(sz_d, [[MD_IN, 128], [1, MD_IN]],
                                      i * 128 * MD_IN)).then_inc(sld, 16)
                sync.dma_start(out=cmt[:, b * MNH:(b + 1) * MNH],
                               in_=ap(cumt_d, [[MNH, 128], [1, MNH]],
                                      i * 128 * MNH)).then_inc(sld, 16)
                sync.dma_start(out=crn[:, b * MNH:(b + 1) * MNH],
                               in_=ap(crun_d, [[MNH, 1], [1, MNH]], i * MNH)
                               ).then_inc(sld, 16)
                sync.dma_start(out=cfb[:, b * 128:(b + 1) * 128],
                               in_=ap(cfm_d, [[NT, 128], [1, 128]], i * 128)
                               ).then_inc(sld, 16)

        @block.gpsimd
        def _(g):
            ph.load_consts(g, s_ld0)
            g.dma_start(out=idb[:, :], in_=identb_d[:, :]).then_inc(s_ld0, 16)
            g.dma_start(out=onesr[:, :], in_=onesr_d[:, :]).then_inc(s_ld0, 16)
            g.dma_start(out=nwb[:, :], in_=ap(nw_d, [[0, 128], [1, MD_IN]])
                        ).then_inc(s_ld0, 16)
            g.dma_start(out=sinf[:, :], in_=sin_d[:, :]).then_inc(s_ld0, 16)
            for k in range(16):
                g.dma_start(out=wosb[:, k * 1024:(k + 1) * 1024],
                            in_=ap(woT_d, [[D, 128], [1, D]], k * 128 * D)
                            ).then_inc(s_w, 16)
            for i in range(NTILE):
                g.wait_ge(chv, ms.get(f"v:vo_{i}", 0))
                g.dma_start(out=ap(vout_o, [[D, 128], [1, D]], i * 128 * D),
                            in_=vo[:, :]).then_inc(s_st, 16)
            g.wait_ge(s_st, 16 * NTILE)

        @block.scalar
        def _(sc0):
            sc = Ch(sc0, chs, ms, "s")
            sc.wait_ge(s_ld0, 16 * 5)
            sc.copy(sinb[:, :], sinf[:, :])
            sc.ms("sinb")
            for i in range(NTILE):
                b = i % NB
                sc.wait_ge(chv, ms.get(f"v:ecfs_{i}", 0))
                sc.activation(ecf[:, :], ecf[:, :], AF.Exp)
                sc.ms(f"ecf_{i}")
                sc.wait_ge(chv, ms.get(f"v:y2_{i}", 0))
                sc.activation(sq[:, :], t2[:, :], AF.Square)
                sc.ms(f"sq_{i}")
                sc.wait_ge(chv, ms.get(f"v:var_{i}", 0))
                sc.activation(stats[:, 1:2], stats[:, 0:1], AF.Sqrt, bias=C(1e-5))
                sc.ms(f"sd_{i}")
                sc.wait_ge(s_mm, ms.get(f"t:tr_{i}", 0))
                sc.copy(yfm[:, :], ptr[:, :])
                sc.ms(f"yfm_{i}")

        @block.vector
        def _(v0):
            v = Ch(v0, chv, ms, "v")
            v.wait_ge(s_ld0, 16 * 5)
            for i in range(NTILE):
                b = i % NB
                v.wait_ge(s_lda if b == 0 else s_ldb, 16 * 5 * (i // NB + 1))
                v.wait_ge(s_mm, ms.get(f"t:pcr_{i}", 0))
                if i > 0:
                    v.wait_ge(chs, ms.get(f"s:ecf_{i - 1}", 0))
                v.tensor_tensor(ecf[:, :], cmt[:, b * MNH:(b + 1) * MNH],
                                pm2[:, 0:MNH], ALU.add)
                v.ms(f"ecfs_{i}")
                v.wait_ge(chs, ms.get(f"s:ecf_{i}", 0))
                v.wait_ge(s_mm, ms.get(f"t:corr_{i}", 0))
                if i > 0:
                    v.wait_ge(chs, ms.get(f"s:sq_{i - 1}", 0))
                v.tensor_tensor(t2[:, :], pc[:, :],
                                ap(ecf, [[MNH, 128], [1, MNH], [0, 64]]), ALU.mult)
                v.tensor_tensor(t2[:, :], yt[:, b * MD_IN:(b + 1) * MD_IN],
                                t2[:, :], ALU.add)
                v.tensor_tensor(t2[:, :], t2[:, :],
                                szt[:, b * MD_IN:(b + 1) * MD_IN], ALU.mult)
                v.ms(f"y2_{i}")
                v.wait_ge(chs, ms.get(f"s:sq_{i}", 0))
                v.reduce_sum(stats[:, 0:1], sq[:, :], axis=AX.X)
                v.tensor_scalar(stats[:, 0:1], stats[:, 0:1], C(1.0 / MD_IN), None,
                                ALU.mult)
                v.ms(f"var_{i}")
                v.wait_ge(chs, ms.get(f"s:sd_{i}", 0))
                v.reciprocal(stats[:, 2:3], stats[:, 1:2])
                if i > 0:
                    v.wait_ge(s_mm, ms.get(f"t:tr_{i - 1}", 0))
                v.tensor_scalar(t2[:, :], t2[:, :], stats[:, 2:3], None, ALU.mult)
                v.tensor_tensor(ynb[:, :], t2[:, :], nwb[:, :], ALU.mult)
                v.ms(f"ynb_{i}")
                v.wait_ge(s_mm, ms.get(f"t:wo_{i}", 0))
                if i > 0:
                    v.wait_ge(s_st, 16 * i)
                v.tensor_copy(vo[:, :], pc[:, 0:1024])
                v.ms(f"vo_{i}")

        @block.tensor
        def _(t):
            t.wait_ge(s_w, 16 * 16)
            t.wait_ge(s_ld0, 16 * 5)
            t.wait_ge(chs, ms.get("s:sinb", 0))
            for i in range(NTILE):
                b = i % NB
                t.wait_ge(s_lda if b == 0 else s_ldb, 16 * 5 * (i // NB + 1))
                if i > 0:
                    t.wait_ge(chv, ms.get(f"v:ecfs_{i - 1}", 0))
                mi = t.matmul(pm2[:, 0:MNH], onesr[:, :], crn[:, b * MNH:(b + 1) * MNH],
                              start=True, stop=True)
                pe_ms(mi, f"pcr_{i}")
                if i > 0:
                    t.wait_ge(chv, ms.get(f"v:vo_{i - 1}", 0))
                mi = None
                for n in range(4):
                    mi = t.matmul(pc[:, n * 512:(n + 1) * 512],
                                  cfb[:, b * 128:(b + 1) * 128],
                                  sinb[:, n * 512:(n + 1) * 512],
                                  start=True, stop=True)
                pe_ms(mi, f"corr_{i}")
                t.wait_ge(chv, ms.get(f"v:ynb_{i}", 0))
                if i > 0:
                    t.wait_ge(chs, ms.get(f"s:yfm_{i - 1}", 0))
                mi = None
                for blk in range(16):
                    mi = t.transpose(ptr[:, blk * 128:(blk + 1) * 128],
                                     ynb[:, blk * 128:(blk + 1) * 128], idb[:, :])
                pe_ms(mi, f"tr_{i}")
                t.wait_ge(chs, ms.get(f"s:yfm_{i}", 0))
                t.wait_ge(chv, ms.get(f"v:y2_{i}", 0))
                mi = None
                for n in range(2):
                    for k in range(16):
                        mi = t.matmul(pc[:, n * 512:(n + 1) * 512],
                                      yfm[:, k * 128:(k + 1) * 128],
                                      wosb[:, k * 1024 + n * 512:k * 1024 + (n + 1) * 512],
                                      start=(k == 0), stop=(k == 15))
                pe_ms(mi, f"wo_{i}")

    return ph


def build_M2():
    return build_twice(_build_M2)


def _build_FD(ms):
    ph = Phase("FD")
    nc = ph.nc
    C = ph.C

    xo_d = ph.inp("xo", (NT, D))
    vin_d = ph.inp("vin", (NT, D))
    xo2_o = ph.out("xo2", (NT, D))
    u2_o = ph.out("u2", (1 + NT, D))
    u2b_o = ph.out("u2_bf", (1 + NT, D), BF16)

    NB = 2
    xot = nc.alloc_sbuf_tensor("xot", [128, NB * D], F32)
    vt = nc.alloc_sbuf_tensor("vt", [128, NB * D], F32)
    sq = nc.alloc_sbuf_tensor("sq", [128, D], F32)
    st_ = nc.alloc_sbuf_tensor("st_", [128, 16], F32)
    lnv = nc.alloc_sbuf_tensor("lnv", [128, D], F32)
    xo2 = nc.alloc_sbuf_tensor("xo2_s", [128, D], F32)
    u2 = nc.alloc_sbuf_tensor("u2_s", [128, D], F32)
    u2b = nc.alloc_sbuf_tensor("u2b", [128, D], BF16)

    with (
        nc.semaphore("s_lda") as s_lda,
        nc.semaphore("s_ldb") as s_ldb,
        nc.semaphore("s_ld0") as s_ld0,
        nc.semaphore("st_x") as st_x,
        nc.semaphore("st_u") as st_u,
        nc.semaphore("st_ub") as st_ub,
        nc.semaphore("chv") as chv,
        nc.semaphore("chs") as chs,
        nc.Block() as block,
    ):
        @block.sync
        def _(sync):
            ph.load_consts(sync, s_ld0)
            for i in range(NTILE):
                b = i % NB
                if i >= NB:
                    sync.wait_ge(chv, ms.get(f"v:xo2_{i - NB}", 0))
                    sync.wait_ge(chs, ms.get(f"s:sq1_{i - NB + 1}", 0))
                sld = s_lda if b == 0 else s_ldb
                sync.dma_start(out=vt[:, b * D:(b + 1) * D],
                               in_=ap(vin_d, [[D, 128], [1, D]], i * 128 * D)
                               ).then_inc(sld, 16)
                sync.dma_start(out=xot[:, b * D:(b + 1) * D],
                               in_=ap(xo_d, [[D, 128], [1, D]], i * 128 * D)
                               ).then_inc(sld, 16)

        @block.gpsimd
        def _(g):
            for i in range(NTILE):
                g.wait_ge(chv, ms.get(f"v:xo2_{i}", 0))
                g.dma_start(out=ap(xo2_o, [[D, 128], [1, D]], i * 128 * D),
                            in_=xo2[:, :]).then_inc(st_x, 16)
                g.wait_ge(chv, ms.get(f"v:u2_{i}", 0))
                g.dma_start(out=ap(u2_o, [[D, 128], [1, D]], (1 + i * 128) * D),
                            in_=u2[:, :]).then_inc(st_u, 16)
                g.wait_ge(chs, ms.get(f"s:u2b_{i}", 0))
                g.dma_start(out=ap(u2b_o, [[D, 128], [1, D]], (1 + i * 128) * D),
                            in_=u2b[:, :]).then_inc(st_ub, 16)
            g.wait_ge(st_x, 16 * NTILE)
            g.wait_ge(st_u, 16 * NTILE)
            g.wait_ge(st_ub, 16 * NTILE)

        @block.scalar
        def _(sc0):
            sc = Ch(sc0, chs, ms, "s")
            sc.wait_ge(s_ld0, 16)
            for i in range(NTILE):
                b = i % NB
                sc.wait_ge(s_lda if b == 0 else s_ldb, 16 * 2 * (i // NB + 1))
                if i > 0:
                    sc.wait_ge(chv, ms.get(f"v:lnv_{i - 1}", 0))
                sc.activation(sq[:, :], vt[:, b * D:(b + 1) * D], AF.Square)
                sc.ms(f"sq1_{i}")
                sc.wait_ge(chv, ms.get(f"v:var1_{i}", 0))
                sc.activation(st_[:, 6:7], st_[:, 5:6], AF.Sqrt, bias=C(1e-5))
                sc.ms(f"sd1_{i}")
                sc.wait_ge(chv, ms.get(f"v:xo2_{i}", 0))
                sc.activation(sq[:, :], xo2[:, :], AF.Square)
                sc.ms(f"sq2_{i}")
                sc.wait_ge(chv, ms.get(f"v:var2_{i}", 0))
                sc.activation(st_[:, 6:7], st_[:, 5:6], AF.Sqrt, bias=C(1e-5))
                sc.ms(f"sd2_{i}")
                sc.wait_ge(chv, ms.get(f"v:u2_{i}", 0))
                if i > 0:
                    sc.wait_ge(st_ub, 16 * i)
                sc.copy(u2b[:, :], u2[:, :])
                sc.ms(f"u2b_{i}")

        @block.vector
        def _(v0):
            v = Ch(v0, chv, ms, "v")
            for i in range(NTILE):
                b = i % NB
                v.wait_ge(s_lda if b == 0 else s_ldb, 16 * 2 * (i // NB + 1))

                def ln_stats(src, sqname, sdname):
                    v.reduce_sum(st_[:, 0:1], src, axis=AX.X)
                    v.wait_ge(chs, ms.get(f"s:{sqname}_{i}", 0))
                    v.reduce_sum(st_[:, 1:2], sq[:, :], axis=AX.X)
                    v.tensor_scalar(st_[:, 4:5], st_[:, 1:2], C(1.0 / D), None, ALU.mult)
                    v.tensor_scalar(st_[:, 2:3], st_[:, 0:1], C(1.0 / D), None, ALU.mult)
                    v.tensor_tensor(st_[:, 3:4], st_[:, 2:3], st_[:, 2:3], ALU.mult)
                    v.tensor_tensor(st_[:, 5:6], st_[:, 4:5], st_[:, 3:4], ALU.subtract)
                    v.ms(f"var{sdname}_{i}")
                    v.wait_ge(chs, ms.get(f"s:sd{sdname}_{i}", 0))
                    v.reciprocal(st_[:, 7:8], st_[:, 6:7])
                    v.tensor_scalar(st_[:, 9:10], st_[:, 0:1], C(-1.0 / D), None,
                                    ALU.mult)

                ln_stats(vt[:, b * D:(b + 1) * D], "sq1", "1")
                v.tensor_scalar(lnv[:, :], vt[:, b * D:(b + 1) * D],
                                st_[:, 9:10], st_[:, 7:8], ALU.add, ALU.mult)
                v.ms(f"lnv_{i}")
                if i > 0:
                    v.wait_ge(st_x, 16 * i)
                v.tensor_tensor(xo2[:, :], xot[:, b * D:(b + 1) * D], lnv[:, :],
                                ALU.add)
                v.ms(f"xo2_{i}")
                ln_stats(xo2[:, :], "sq2", "2")
                if i > 0:
                    v.wait_ge(st_u, 16 * i)
                    v.wait_ge(chs, ms.get(f"s:u2b_{i - 1}", 0))
                v.tensor_scalar(u2[:, :], xo2[:, :], st_[:, 9:10], st_[:, 7:8],
                                ALU.add, ALU.mult)
                v.ms(f"u2_{i}")

    return ph


def build_FD():
    return build_twice(_build_FD)


from concourse.bass_utils import run_bass_kernel_spmd

NCORE = 8
B4, TFULL = 4, 8192

_PHASES = {}


def get_phase(name):
    if name not in _PHASES:
        if name == "FA":
            _PHASES[name] = build_FA()
        elif name == "FB":
            _PHASES[name] = build_FB()
        elif name == "FCg":
            _PHASES[name] = build_FC(act="gelu", rms=False, eps=1e-5, shift=True)
        elif name == "FCs":
            _PHASES[name] = build_FC(act="silu", rms=True, eps=1e-6, shift=False)
        elif name == "G1":
            _PHASES[name] = build_G1()
        elif name == "G2":
            _PHASES[name] = build_G2()
        elif name == "M1":
            _PHASES[name] = build_M1()
        elif name == "M2":
            _PHASES[name] = build_M2()
        elif name == "FD":
            _PHASES[name] = build_FD()
    return _PHASES[name]


def run_phase(name, per_core_inputs):
    ph = get_phase(name)
    cst = ph.cst_array()
    in_maps = []
    for c in range(NCORE):
        d = dict(per_core_inputs[c])
        d["cstt"] = cst
        for k, (shape, dt) in ph.inputs.items():
            assert k in d, f"{name}: missing input {k}"
            a = d[k]
            assert tuple(a.shape) == shape, f"{name}: {k} shape {a.shape} != {shape}"
        in_maps.append(d)
    res = run_bass_kernel_spmd(ph.nc, in_maps, core_ids=list(range(NCORE)))
    return res.results


def _zeros(shape, dtype=np.float32):
    return np.zeros(shape, dtype)


def _route_halo_row(xi_list, nrows=1, dtype=np.float32):
    out = []
    for c in range(NCORE):
        a = xi_list[c].copy()
        if c % 2 == 1:
            a[0] = xi_list[c - 1][NT]
        else:
            a[0] = 0
        out.append(a)
    return out


def kernel(**inputs):
    I = inputs
    x0 = np.asarray(I["L0_attn_hidden_states"], np.float32)
    xo = [np.ascontiguousarray(x0[c // 2, (c % 2) * NT:(c % 2 + 1) * NT])
          for c in range(NCORE)]
    eye_b = np.eye(128, dtype=BF)
    eye_f = np.eye(128, dtype=np.float32)
    onesrow = np.ones((1, 128), np.float32)
    g1c = host_g1_consts()
    m1c = host_m1_consts()
    dj = host_dj()
    tpos = [host_tpos((c % 2) * NT) for c in range(NCORE)]
    masks = [host_masks(c % 2 == 0) for c in range(NCORE)]

    r = run_phase("G1", [{
        "xo": xo[c], "Wq": I["L0_attn_Wq"], "Wk": I["L0_attn_Wk"],
        "Wv": I["L0_attn_Wv"], "Wg": I["L0_attn_Wg"], "Wgk1": I["L0_attn_Wgk1"],
        "Wgk2": I["L0_attn_Wgk2"], "bgk2": I["L0_attn_bgk2"],
        "attn_norm_w": I["L0_attn_attn_norm_w"], "identb": eye_b, **g1c,
    } for c in range(NCORE)])
    g1o = r
    r = run_phase("G2", [{
        "u": g1o[c]["u"], "o_raw": g1o[c]["o_raw"], "sg": g1o[c]["sg"],
        "qefm": g1o[c]["qefm"], "brun": g1o[c]["brun"],
        "S_in": (g1o[c - 1]["S_out"] if c % 2 == 1 else _zeros((128, 1024))),
        "gnorm_w": I["L0_attn_gnorm_w"], "Wo": I["L0_attn_Wo"], "identb": eye_b,
    } for c in range(NCORE)])
    x2g = [r[c]["x2"] for c in range(NCORE)]
    w1r_g, b1r_g = host_w1r(I["L0_attn_W_up"], I["L0_attn_W_gate"],
                            np.zeros(2816, np.float32), np.zeros(2816, np.float32))
    w2r_g = host_w2r(I["L0_attn_W_down"])
    r = run_phase("FCs", [{
        "x2": x2g[c], "W1r": w1r_g, "b1r": b1r_g, "W2r": w2r_g,
        "b2": np.zeros(1024, np.float32), "lng": I["L0_attn_mlp_norm_w"],
        "lnb": np.zeros(1024, np.float32), "identb": eye_b,
    } for c in range(NCORE)])
    vgla = [r[c]["xi3"][1:] for c in range(NCORE)]
    r = run_phase("FD", [{"xo": xo[c], "vin": vgla[c]} for c in range(NCORE)])
    xo = [r[c]["xo2"] for c in range(NCORE)]
    xi = [r[c]["u2"] for c in range(NCORE)]
    xib = [r[c]["u2_bf"] for c in range(NCORE)]

    def ffn_slot(pfx, xo, xi, xib):
        for L in range(6):
            Wqkv = np.ascontiguousarray(I[pfx + "Wqkv"][L])
            qs, ks = I[pfx + "q_scale"][L], I[pfx + "k_scale"][L]
            qsks = np.stack([qs[:32], qs[32:], ks[:32], ks[32:]])
            xib_h = _route_halo_row(xib)
            r = run_phase("FA", [{
                "xi_bf": xib_h[c], "Wqkv": Wqkv, "qsks": qsks,
                "tpos": tpos[c], "dj": dj,
            } for c in range(NCORE)])
            qtm = [r[c]["qtm"] for c in range(NCORE)]
            ktm = [r[c]["ktm"] for c in range(NCORE)]
            vtm = [r[c]["vtm"] for c in range(NCORE)]
            ktm_h, vtm_h = [], []
            for c in range(NCORE):
                if c % 2 == 1:
                    kh, vh = ktm[c - 1][-WIN:], vtm[c - 1][-WIN:]
                else:
                    kh = np.zeros((WIN, D), ktm[c].dtype)
                    vh = np.zeros((WIN, HEADS * 65), vtm[c].dtype)
                ktm_h.append(np.concatenate([kh, ktm[c]], 0))
                vtm_h.append(np.concatenate([vh, vtm[c]], 0))
            r = run_phase("FB", [{
                "qtm": qtm[c], "ktm_h": ktm_h[c], "vtm_h": vtm_h[c],
                "xi": xi[c], "Wo": np.ascontiguousarray(I[pfx + "Wo"][L]),
                "masks": masks[c], "identf": eye_f, "onesrow": onesrow,
            } for c in range(NCORE)])
            x2 = _route_halo_row([r[c]["x2"] for c in range(NCORE)])
            W1 = I[pfx + "W1"][L]
            b1 = I[pfx + "b1"][L]
            w1r, b1r = host_w1r(W1[:, :2730], W1[:, 2730:], b1[:2730], b1[2730:])
            r = run_phase("FCg", [{
                "x2": x2[c], "W1r": w1r, "b1r": b1r,
                "W2r": host_w2r(I[pfx + "W2"][L]), "b2": I[pfx + "b2"][L],
                "lng": I[pfx + "ln_g"][L], "lnb": I[pfx + "ln_b"][L],
                "identb": eye_b,
            } for c in range(NCORE)])
            xi = [r[c]["xi3"] for c in range(NCORE)]
            xib = [r[c]["xi3_bf"] for c in range(NCORE)]
        r = run_phase("FD", [{"xo": xo[c], "vin": xi[c][1:]} for c in range(NCORE)])
        return ([r[c]["xo2"] for c in range(NCORE)],
                [r[c]["u2"] for c in range(NCORE)],
                [r[c]["u2_bf"] for c in range(NCORE)])

    xo, xi, xib = ffn_slot("L0_ffn_", xo, xi, xib)

    ub3 = []
    for c in range(NCORE):
        a = np.zeros((3 + NT, D), dtype=BF)
        a[3:] = xib[c][1:]
        if c % 2 == 1:
            a[0:3] = xib[c - 1][NT - 2:NT + 1]
        ub3.append(a)
    r = run_phase("M1", [{
        "u_bf": ub3[c], "W_inT": np.ascontiguousarray(I["L1_attn_W_in"].T),
        "conv_wT": np.ascontiguousarray(I["L1_attn_conv_w"].T),
        "conv_b": I["L1_attn_conv_b"], "dt_bias": I["L1_attn_dt_bias"],
        "A_log": I["L1_attn_A_log"], "D_h": I["L1_attn_D"],
        "identb": eye_b, "identf": eye_f, **m1c,
    } for c in range(NCORE)])
    m1o = r
    r = run_phase("M2", [{
        "y_raw": m1o[c]["y_raw"], "sz": m1o[c]["sz"], "cumt": m1o[c]["cumt"],
        "crun": m1o[c]["crun"], "Cfm": m1o[c]["Cfm"],
        "S_in": (m1o[c - 1]["S_out"] if c % 2 == 1 else _zeros((128, MD_IN))),
        "norm_w": I["L1_attn_norm_w"],
        "W_outT": np.ascontiguousarray(I["L1_attn_W_out"].T),
        "identb": eye_b, "onesrow": onesrow,
    } for c in range(NCORE)])
    vmam = [r[c]["vout"] for c in range(NCORE)]
    r = run_phase("FD", [{"xo": xo[c], "vin": vmam[c]} for c in range(NCORE)])
    xo = [r[c]["xo2"] for c in range(NCORE)]
    xi = [r[c]["u2"] for c in range(NCORE)]
    xib = [r[c]["u2_bf"] for c in range(NCORE)]

    xo, xi, xib = ffn_slot("L1_ffn_", xo, xi, xib)

    out = np.empty((B4, TFULL, D), np.float32)
    for c in range(NCORE):
        out[c // 2, (c % 2) * NT:(c % 2 + 1) * NT] = xo[c]
    return out
